# version 12
# baseline (speedup 1.0000x reference)
import numpy as np
import concourse.bass as bass
import concourse.mybir as mybir
from concourse.bass_utils import run_bass_kernel_spmd

F32 = mybir.dt.float32
BF16 = mybir.dt.bfloat16
ALU = mybir.AluOpType
AF = mybir.ActivationFunctionType
AX = mybir.AxisListType

D = 1024
H = 16
HD = 64
TC = 2048
NA = 8
TA = 256
NF = 4
TF = 512
KC = 8
MASKC = 200.0
EPS = 1e-5
ENGS = ("sync", "gpsimd", "tensor", "vector", "scalar")


def _raw(t):
    return t.tensor if hasattr(t, "tensor") else t


class Build:

    def __init__(self, semnames):
        self.cnt = {k: 0 for k in semnames}
        self.q = {e: [] for e in ENGS}

    def W(self, eng, *toks):
        for t in toks:
            if t is not None:
                self.q[eng].append(("w", t[0], t[1]))

    def do(self, eng, maker, sem=None, n=1):
        if sem is None or n == 0:
            self.q[eng].append(("i", maker, None, 0))
            return None
        self.cnt[sem] += n
        self.q[eng].append(("i", maker, sem, n))
        return (sem, self.cnt[sem])

    def tok(self, sem):
        return (sem, self.cnt[sem])

    def replay(self, eng, E, sems):
        for item in self.q[eng]:
            if item[0] == "w":
                E.wait_ge(sems[item[1]], item[2])
            else:
                _, maker, sem, n = item
                inst = maker(E)
                if sem is not None:
                    inst.then_inc(sems[sem], n)


def build_nc():
    nc = bass.Bass("TRN2", target_bir_lowering=False, debug=False)

    d_in = {}

    def din(name, shape):
        d_in[name] = nc.dram_tensor(name, list(shape), F32, kind="ExternalInput").ap()
        return d_in[name]

    x_att = din("x_att", [TC, D])
    x_res = din("x_res", [TC, D])
    wqk = din("wqk", [16, 128, 1024])
    wv = din("wv", [8, 128, 1024])
    wproj = din("wproj", [8, 128, 1024])
    wup = din("wup", [8, 4, 128, 1024])
    wdown = din("wdown", [8, 4, 128, 1024])
    wsup = din("wsup", [16, 128, 1024])
    wsdown = din("wsdown", [8, 2, 128, 1024])
    wgate = din("wgate", [128, 64])
    bproj = din("bproj", [128, 8])
    gbias = din("gbias", [128, 8])
    identw = din("identw", [128, 128])
    kmask = din("kmask", [8, 16 * TA])
    qmask = din("qmask", [8, 16 * TA])
    selm = din("selm", [8, 1024])
    out_d = nc.dram_tensor("out", [TC, D], F32, kind="ExternalOutput").ap()

    def scr_dram(name, shape):
        return nc.dram_tensor(name, list(shape), BF16, kind="Internal").ap()
    sqk = scr_dram("sqk", [16, 128, 2048])
    sv = scr_dram("sv", [8, 128, 2048])
    sproj = scr_dram("sproj", [8, 128, 2048])
    sup_s = scr_dram("sup_s", [8, 4, 128, 2048])
    sdn_s = scr_dram("sdn_s", [8, 4, 128, 2048])
    ssup = scr_dram("ssup", [16, 128, 2048])
    ssdn = scr_dram("ssdn", [8, 2, 128, 2048])

    from contextlib import ExitStack
    with ExitStack() as _st:
        acc = _st.enter_context(nc.sbuf_tensor("acc", [128, 8 * 2048], F32))
        ident = _st.enter_context(nc.sbuf_tensor("ident", [128, 128], F32))
        bproj_sb = _st.enter_context(nc.sbuf_tensor("bproj_sb", [128, 8], F32))
        gbias_sb = _st.enter_context(nc.sbuf_tensor("gbias_sb", [128, 8], F32))
        gate_sb = _st.enter_context(nc.sbuf_tensor("gate_sb", [128, 64], F32))
        wb = _st.enter_context(nc.sbuf_tensor("wb", [128, 4 * 2048], BF16))
        wbig = _st.enter_context(nc.sbuf_tensor("wbig", [128, 2 * 2048], BF16))
        gate_hi = _st.enter_context(nc.sbuf_tensor("gate_hi", [128, 64], BF16))
        gate_lo = _st.enter_context(nc.sbuf_tensor("gate_lo", [128, 64], BF16))
        stats = _st.enter_context(nc.sbuf_tensor("stats", [128, 48], F32))
        mv = _st.enter_context(nc.sbuf_tensor("mv", [128, 8], F32))
        rstd = _st.enter_context(nc.sbuf_tensor("rstd", [128, 4], F32))
        rzb = _st.enter_context(nc.sbuf_tensor("rzb", [128, 4], F32))
        ARF = 21280
        AR = _st.enter_context(nc.sbuf_tensor("AR", [128, ARF], F32))
        _off = [0]

        def take(parts, elems):
            o = _off[0]
            _off[0] += elems
            assert _off[0] <= ARF, "arena overflow"
            return AR[0:parts, o:o + elems], o
        ARB = 16384
        ARB_T = _st.enter_context(nc.sbuf_tensor("ARB", [128, ARB], BF16))
        _off16 = [0]

        def take16(parts, elems_bf16):
            o = _off16[0]
            _off16[0] += elems_bf16
            assert _off16[0] <= ARB, "bf16 arena overflow"
            return ARB_T[0:parts, o:o + elems_bf16], o
        xa, _ = take(128, 2 * 1024)
        oddst, _ = take(128, 8 * TA)
        wsf, _ = take(128, 2 * 1024)
        xnf_hi, _ = take16(128, 8 * TA)
        xnf_lo, _ = take16(128, 8 * TA)
        wvb, _ = take16(128, 2 * 2048)
        q_sr, _ = take(72, 16 * TA)
        k_sr, _ = take(72, 16 * TA)
        v_tok, c_vtok = take(128, 2 * 1024)
        v_pv, c_vpv = take(128, 2 * 16 * 65)
        Pt, _ = take(128, 512)
        o_sb, _ = take(128, 1024)
        o_mq_hi, _ = take16(128, 8 * TA)
        o_mq_lo, _ = take16(128, 8 * TA)
        spl, _ = take16(128, 2 * 2048)
        _off[0] = 0
        _off16[0] = 0
        xr, _ = take(128, 4 * 1024)
        wsf_f, _ = take(128, 2 * 1024)
        xnr, _ = take(128, 4 * 1024)
        xnff_hi, _ = take16(128, 8 * TF)
        xnff_lo, _ = take16(128, 8 * TF)
        lgt, _ = take(128, 32)
        rt0, _ = take(128, 32)
        rt1, _ = take(128, 64)
        rt2, _ = take(128, 32)
        rt3, _ = take(128, 32)
        rt4, _ = take(128, 32)
        rt5, _ = take(128, 32)
        cwfm, _ = take(128, TF)
        selm_sb, _ = take(8, 1024)
        scr_hi, _ = take16(128, 8 * TF)
        scr_lo, _ = take16(128, 8 * TF)
        scr32, _ = take(128, 2 * TF)
        routed, _ = take(128, 8 * TF)
        xres, _ = take(128, 2 * 1024)
        outp, _ = take(128, 2 * 1024)
        p0 = _st.enter_context(nc.psum_tensor("p0", [128, 512], F32))
        p1 = _st.enter_context(nc.psum_tensor("p1", [128, 512], F32))
        p2 = _st.enter_context(nc.psum_tensor("p2", [128, 512], F32))
        p3 = _st.enter_context(nc.psum_tensor("p3", [128, 512], F32))
        p4 = _st.enter_context(nc.psum_tensor("p4", [128, 512], F32))
        p5 = _st.enter_context(nc.psum_tensor("p5", [128, 512], F32))
        p6 = _st.enter_context(nc.psum_tensor("p6", [128, 512], F32))
        p7 = _st.enter_context(nc.psum_tensor("p7", [128, 512], F32))
        semnames = ["I", "X", "W0", "W1", "W2", "W3", "WV0", "WV1", "B0", "B1",
                    "SHQ", "SHK", "PV", "PV2", "O0", "O1",
                    "F0", "F1", "SW0", "SW1",
                    "XR0", "XR1", "G0", "G1", "T", "V", "S", "P"]
        sems = {k: _st.enter_context(nc.semaphore("s" + k)) for k in semnames}
        block = _st.enter_context(nc.Block())
        bd = Build(list(sems))
        W = bd.W
        pmm = [p0, p1]
        pv4 = [p4, p5, p6, p7]
        pv4_last = [None] * 4

        for ap_src, dst in ((identw, ident), (bproj, bproj_sb), (gbias, gbias_sb),
                            (wgate, gate_sb)):
            bd.do("sync", lambda E, s=ap_src, d=dst: E.dma_start(out=d[:], in_=s), "I", 16)
        bd.do("sync", lambda E: E.dma_start(out=k_sr[64:72, :], in_=kmask), "I", 16)
        bd.do("sync", lambda E: E.dma_start(out=q_sr[64:72, :], in_=qmask), "I", 16)
        t_init = bd.tok("I")
        W("tensor", t_init)
        W("vector", t_init)
        W("scalar", t_init)
        v_ones = bd.do("vector", lambda E: E.memset(v_pv[:], 1.0), "V", 1)
        tg1 = bd.do("scalar", lambda E: E.copy(gate_hi[:], gate_sb[:]), "S", 1)
        W("vector", tg1)
        t_gate = bd.do("vector", lambda E:
                       E.tensor_sub(gate_lo[:], gate_sb[:], gate_hi[:]), "V", 1)

        last = {
            "xa": None, "xnf": None, "q_sr": None, "k_sr": None, "oddst": None,
            "v_tok": None, "v_pv": None, "Pt": None, "o_sb": None, "o_mq": None,
            "pmm0": None, "pmm1": None, "p2": None, "p3": None,
            "xr": None, "xnr0": None, "xnff": None,
            "scr": None, "routed": None, "cwfm": None,
        }
        t_wb_prev = [None] * 4
        t_wv_prev = [None, None]
        t_wbig_prev = [None, None]
        t_outp_prev = [None, None]
        t_wsf_prev = [None] * 4
        t_wb_scrw = [None, None]
        t_spl_scrw = [None, None]

        def load_w(buf, sl, wsem, fp32_src, scratch_blk, first, reuse_tok,
                   sw_toks=None):
            if first:
                fs = sl % 2
                W("sync", t_wsf_prev[fs])
                dfi = bd.do("sync", lambda E, fs=fs, src=fp32_src:
                            E.dma_start(out=wsf[:, 1024 * fs:1024 * fs + 1024],
                                        in_=src), "F%d" % fs, 16)
                W("scalar", dfi, t_wb_scrw[fs], reuse_tok)
                thi = bd.do("scalar", lambda E, fs=fs, sl=sl, b=buf:
                            E.copy(b[:, 2048 * sl:2048 * sl + 1024],
                                   wsf[:, 1024 * fs:1024 * fs + 1024]), "S", 1)
                W("vector", thi)
                tlo = bd.do("vector", lambda E, fs=fs, sl=sl, b=buf:
                            E.tensor_sub(b[:, 2048 * sl + 1024:2048 * sl + 2048],
                                         wsf[:, 1024 * fs:1024 * fs + 1024],
                                         b[:, 2048 * sl:2048 * sl + 1024]), "V", 1)
                t_wsf_prev[fs] = tlo
                W("gpsimd", tlo)
                tsw = bd.do("gpsimd", lambda E, sl=sl, b=buf, dst=scratch_blk:
                            E.dma_start(out=dst, in_=b[:, 2048 * sl:2048 * sl + 2048]),
                            "SW%d" % (sl % 2), 16)
                t_wb_scrw[fs] = tsw
                return tlo
            else:
                W("sync", reuse_tok, *(sw_toks or ()))
                dw = bd.do("sync", lambda E, sl=sl, b=buf, src=scratch_blk:
                           E.dma_start(out=b[:, 2048 * sl:2048 * sl + 2048],
                                       in_=src), wsem, 16)
                return dw

        def presplit(fp32_src, scratch_blk, pi):
            fs = pi % 2
            ps = pi % 2
            W("sync", t_wsf_prev[fs])
            dfi = bd.do("sync", lambda E, fs=fs, src=fp32_src:
                        E.dma_start(out=wsf[:, 1024 * fs:1024 * fs + 1024],
                                    in_=src), "F%d" % fs, 16)
            W("scalar", dfi, t_spl_scrw[ps])
            thi = bd.do("scalar", lambda E, fs=fs, ps=ps:
                        E.copy(spl[:, 2048 * ps:2048 * ps + 1024],
                               wsf[:, 1024 * fs:1024 * fs + 1024]), "S", 1)
            W("vector", thi)
            tlo = bd.do("vector", lambda E, fs=fs, ps=ps:
                        E.tensor_sub(spl[:, 2048 * ps + 1024:2048 * ps + 2048],
                                     wsf[:, 1024 * fs:1024 * fs + 1024],
                                     spl[:, 2048 * ps:2048 * ps + 1024]), "P", 1)
            t_wsf_prev[fs] = tlo
            W("gpsimd", tlo)
            t_spl_scrw[ps] = bd.do("gpsimd", lambda E, ps=ps, dst=scratch_blk:
                                   E.dma_start(out=dst,
                                               in_=spl[:, 2048 * ps:2048 * ps + 2048]),
                                   "SW%d" % ps, 16)

        def wsl(buf, sl, kc, lo=False):
            o = 2048 * sl + (1024 if lo else 0) + 128 * kc
            return buf[:, o:o + 128]

        def mm_triple(psum_ap, a_hi, a_lo, b_hi, b_lo, first_flag, stop_flag,
                      inc=None):
            if inc is None:
                inc = stop_flag
            ops = [(a_hi, b_hi), (a_hi, b_lo), (a_lo, b_hi)]
            mmi = None
            for i, (wa, xa_) in enumerate(ops):
                mmi = bd.do("tensor", lambda E, wa=wa, xa_=xa_, i=i:
                            E.matmul(psum_ap, wa, xa_,
                                     start=(first_flag and i == 0),
                                     stop=(stop_flag and i == 2)),
                            "T", 1 if (inc and i == 2) else 0)
            return mmi

        def ln_subtile(src_sb, src_off, dst_sb, dst_off, st_i, v_deps):
            W("vector", *v_deps)
            bd.do("vector", lambda E, o=src_off, i=st_i:
                  E.bn_stats(stats[:, 12 * i:12 * i + 6], src_sb[:, o:o + 512]), "V", 1)
            t2 = bd.do("vector", lambda E, o=src_off, i=st_i:
                       E.bn_stats(stats[:, 12 * i + 6:12 * i + 12],
                                  src_sb[:, o + 512:o + 1024]), "V", 1)
            W("vector", t2)
            t3 = bd.do("vector", lambda E, i=st_i:
                       E.bn_aggr(mv[:, 2 * i:2 * i + 2], stats[:, 12 * i:12 * i + 12]),
                       "V", 1)
            W("vector", t3)
            t3b = bd.do("vector", lambda E, i=st_i:
                        E.tensor_scalar_add(rstd[:, i:i + 1],
                                            mv[:, 2 * i + 1:2 * i + 2], EPS), "V", 1)
            W("scalar", t3b)
            t4 = bd.do("scalar", lambda E, i=st_i:
                       E.activation(rstd[:, i:i + 1], rstd[:, i:i + 1], AF.Ln), "S", 1)
            W("vector", t4)
            t4b = bd.do("vector", lambda E, i=st_i:
                        E.tensor_scalar_mul(rstd[:, i:i + 1], rstd[:, i:i + 1], -0.5),
                        "V", 1)
            W("scalar", t4b)
            t5 = bd.do("scalar", lambda E, i=st_i:
                       E.activation(rstd[:, i:i + 1], rstd[:, i:i + 1], AF.Exp), "S", 1)
            W("vector", t5)
            t6 = bd.do("vector", lambda E, so=src_off, do_=dst_off, i=st_i,
                       s=src_sb, d=dst_sb:
                       E.tensor_scalar(d[:, do_:do_ + 1024], s[:, so:so + 1024],
                                       mv[:, 2 * i:2 * i + 1], rstd[:, i:i + 1],
                                       ALU.subtract, ALU.mult), "V", 1)
            return t6

        def transpose_subtile(src_sb, src_off, dst_hi, dst_lo, FT, col_off,
                              v_dep, reuse_tok):
            ev = None
            for c in range(8):
                W("tensor", v_dep, pv4_last[c % 4])
                tt = bd.do("tensor", lambda E, c=c, o=src_off, s=src_sb:
                           E.transpose(pv4[c % 4][:, 0:128],
                                       s[:, o + 128 * c:o + 128 * c + 128],
                                       ident[:]), "T", 1)
                W("scalar", tt, reuse_tok if c == 0 else None)
                eh = bd.do("scalar", lambda E, c=c, co=col_off, d=dst_hi, FT=FT:
                           E.copy(d[:, c * FT + co:c * FT + co + 128],
                                  pv4[c % 4][:, 0:128]), "S", 1)
                W("vector", eh, reuse_tok if c == 0 else None)
                ev = bd.do("vector", lambda E, c=c, co=col_off, dh=dst_hi, dl=dst_lo,
                           FT=FT:
                           E.tensor_sub(dl[:, c * FT + co:c * FT + co + 128],
                                        pv4[c % 4][:, 0:128],
                                        dh[:, c * FT + co:c * FT + co + 128]), "V", 1)
                pv4_last[c % 4] = ev
            return ev

        def att_prep_ln(a):
            W("sync", last["xa"])
            dx = None
            for ss in range(2):
                dx = bd.do("sync", lambda E, ss=ss, a=a:
                           E.dma_start(out=xa[:, 1024 * ss:1024 * ss + 1024],
                                       in_=x_att[256 * a + 128 * ss:
                                                 256 * a + 128 * ss + 128, :]),
                           "X", 16)
            lns = []
            for ss in range(2):
                lns.append(ln_subtile(xa, 1024 * ss, xa, 1024 * ss, ss,
                                      [dx, last["xa"]]))
            return lns

        def att_prep_tr(lns):
            t_xnf = None
            for ss in range(2):
                t_xnf = transpose_subtile(xa, 1024 * ss, xnf_hi, xnf_lo, TA,
                                          128 * ss, lns[ss],
                                          last["xnf"] if ss == 0 else None)
            last["xa"] = bd.tok("T")
            return t_xnf

        def ffn_prep_dma(f, guards_sync=()):
            W("sync", last["xr"], *guards_sync)
            dxr = None
            for tsub in range(4):
                dxr = bd.do("sync", lambda E, f=f, tsub=tsub:
                            E.dma_start(out=xr[:, 1024 * tsub:1024 * tsub + 1024],
                                        in_=x_res[512 * f + 128 * tsub:
                                                  512 * f + 128 * tsub + 128, :]),
                            "X", 16)
            return dxr

        def ffn_prep_lnonly(f, dxr, guards_v=()):
            lns = []
            for tsub in range(4):
                lns.append(ln_subtile(xr, 1024 * tsub, xnr, 1024 * tsub, tsub,
                                      [dxr, last["xnr0"]] + list(guards_v)))
            last["xr"] = lns[3]
            return lns

        def ffn_prep_ln(f, guards_sync=(), guards_v=()):
            dxr = ffn_prep_dma(f, guards_sync)
            return ffn_prep_lnonly(f, dxr, guards_v)

        def ffn_prep_tr(lns):
            t_x = None
            for tsub in range(4):
                t_x = transpose_subtile(xnr, 1024 * tsub, xnff_hi, xnff_lo, TF,
                                        128 * tsub, lns[tsub],
                                        last["xnff"] if tsub == 0 else None)
            last["xnr0"] = bd.tok("T")
            return t_x

        presplit_jobs = []
        for ic in range(16):
            presplit_jobs.append((wsup[ic], ssup[ic]))
        for e in range(8):
            for ic in range(4):
                presplit_jobs.append((wup[e, ic], sup_s[e, ic]))
        for oc in range(8):
            for ep in range(4):
                presplit_jobs.append((wdown[oc, ep], sdn_s[oc, ep]))
        for oc in range(8):
            for ih in range(2):
                presplit_jobs.append((wsdown[oc, ih], ssdn[oc, ih]))
        assert len(presplit_jobs) == 96

        lns0 = att_prep_ln(0)
        t_xnf = att_prep_tr(lns0)
        sw_att = [None]
        lnsf0 = [None]
        dxr_f0 = [None]
        t_xnff_f0 = [None]
        for a in range(NA):
            t_vmm = None
            for kc in range(KC):
                sl = kc % 2
                dwv = load_w(wvb, sl, "WV%d" % sl, wv[kc], sv[kc], a == 0,
                             t_wv_prev[sl], sw_att[0])
                W("tensor", dwv)
                if kc == 0:
                    W("tensor", *pv4_last)
                    W("tensor", last["v_tok"])
                for tsub in range(2):
                    for vc in range(2):
                        t_vmm = mm_triple(
                            pv4[tsub * 2 + vc][:, 0:512],
                            xnf_hi[:, TA * kc + 128 * tsub:TA * kc + 128 * tsub + 128],
                            xnf_lo[:, TA * kc + 128 * tsub:TA * kc + 128 * tsub + 128],
                            wvb[:, 2048 * sl + 512 * vc:2048 * sl + 512 * vc + 512],
                            wvb[:, 2048 * sl + 1024 + 512 * vc:
                                2048 * sl + 1024 + 512 * vc + 512],
                            kc == 0, kc == KC - 1,
                            inc=(tsub == 1 and vc == 1))
                t_wv_prev[sl] = t_vmm
            W("scalar", t_vmm, last["v_tok"])
            for tsub in range(2):
                for vc in range(2):
                    tv = bd.do("scalar", lambda E, tsub=tsub, vc=vc:
                               E.copy(v_tok[:, 1024 * tsub + 512 * vc:
                                             1024 * tsub + 512 * vc + 512],
                                      pv4[tsub * 2 + vc][:, 0:512]), "S", 1)
                    pv4_last[tsub * 2 + vc] = tv
            t_vev = tv
            W("scalar", t_vev, v_ones, last["v_pv"])
            t_vpv = {}
            for g in range(16):
                for tsub in range(2):
                    eng, sem = (("scalar", "PV") if (2 * g + tsub) % 2 == 0
                                else ("scalar", "PV2"))
                    t_vpv[sem] = bd.do(eng, lambda E, g=g, tsub=tsub:
                                       E.dma_start(
                                           out=bass.AP(_raw(AR),
                                                       8 * g * ARF + c_vpv + 1040 * tsub,
                                                       [[ARF, 8], [65, 16], [1, 64]]),
                                           in_=bass.AP(_raw(AR),
                                                       g * 64 + c_vtok + 1024 * tsub,
                                                       [[ARF, 128], [1, 64]])),
                                       sem, 16)
            t_vpv = (t_vpv["PV"], t_vpv["PV2"])
            last["v_tok"] = t_vpv[0]

            lns_next = att_prep_ln(a + 1) if a + 1 < NA else None

            t_shift = {"q": None, "k": None}
            t_evac_qk = None
            for oc in range(16):
                sl = oc % 4
                dw = load_w(wb, sl, "W%d" % sl, wqk[oc], sqk[oc], a == 0,
                            t_wb_prev[sl], sw_att[0])
                W("tensor", dw, t_xnf, last["pmm%d" % (sl % 2)])
                mm = None
                for kc in range(KC):
                    mm = mm_triple(pmm[sl % 2][:, 0:TA],
                                   wsl(wb, sl, kc), wsl(wb, sl, kc, True),
                                   xnf_hi[:, TA * kc:TA * kc + TA],
                                   xnf_lo[:, TA * kc:TA * kc + TA],
                                   kc == 0, kc == KC - 1)
                t_wb_prev[sl] = mm
                dst = q_sr if oc < 8 else k_sr
                c = oc % 8
                W("scalar", mm,
                  last["q_sr"] if oc == 0 else None,
                  t_shift["q"] if oc == 8 else None,
                  last["oddst"] if oc == 0 else None)
                if oc < 8:
                    W("vector", mm,
                      last["q_sr"] if oc == 0 else None,
                      last["oddst"] if oc == 0 else None)
                    e1 = bd.do("vector", lambda E, c=c, sl=sl, dst=dst:
                               E.tensor_scalar_mul(
                                   dst.rearrange("p (t2 h t) -> p t2 h t", t2=2, h=16)
                                      [0:64, :, 2 * c, :],
                                   pmm[sl % 2][0:64, 0:TA].rearrange("p (t2 t) -> p t2 t", t2=2),
                                   0.125), "V", 1)
                    e2 = bd.do("vector", lambda E, c=c, sl=sl:
                               E.tensor_scalar_mul(
                                   oddst.rearrange("p (t2 c t) -> p t2 c t", t2=2, c=8)
                                        [64:128, :, c, :],
                                   pmm[sl % 2][64:128, 0:TA].rearrange("p (t2 t) -> p t2 t", t2=2),
                                   0.125), "V", 1)
                else:
                    e1 = bd.do("scalar", lambda E, c=c, sl=sl, dst=dst:
                               E.copy(dst.rearrange("p (t2 h t) -> p t2 h t", t2=2, h=16)
                                         [0:64, :, 2 * c, :],
                                      pmm[sl % 2][0:64, 0:TA].rearrange("p (t2 t) -> p t2 t", t2=2)),
                               "S", 1)
                    e2 = bd.do("scalar", lambda E, c=c, sl=sl:
                               E.copy(oddst.rearrange("p (t2 c t) -> p t2 c t", t2=2, c=8)
                                           [64:128, :, c, :],
                                      pmm[sl % 2][64:128, 0:TA].rearrange("p (t2 t) -> p t2 t", t2=2)),
                               "S", 1)
                last["pmm%d" % (sl % 2)] = e2
                t_evac_qk = e2
                if a >= 2:
                    pj = 16 * (a - 2) + oc
                    presplit(presplit_jobs[pj][0], presplit_jobs[pj][1], pj)
                if oc in (7, 15):
                    key = "q" if oc == 7 else "k"
                    eng, shsem = ("sync", "SHQ") if oc == 7 else ("scalar", "SHK")
                    W(eng, e2, last["q_sr"])
                    t_shift[key] = bd.do(eng, lambda E, dst=dst:
                                         E.dma_start(
                                             out=dst.rearrange("p (t2 h t) -> p t2 h t",
                                                               t2=2, h=16)[0:64, :, 1::2, :],
                                             in_=oddst[64:128, :]), shsem, 16)
            last["oddst"] = t_shift["k"]

            if a + 1 < NA:
                last["xnf"] = bd.tok("T")
                t_xnf_next = att_prep_tr(lns_next)
            if a == NA - 1:
                tqk7 = bd.tok("T")
                dxr_f0[0] = ffn_prep_dma(0, guards_sync=(tqk7, t_shift["q"],
                                                         t_shift["k"]))

            for tt in range(2):
                for b4 in range(4):
                    W("tensor", t_shift["q"], t_shift["k"], t_evac_qk,
                      last["p3"])
                    mmS = None
                    for slq in range(4):
                        vi = 4 * b4 + slq
                        mmS = bd.do("tensor", lambda E, vi=vi, slq=slq, tt=tt:
                                    E.matmul(p3[:, 128 * slq:128 * slq + 128],
                                             k_sr.rearrange("p (t2 g t) -> p t2 g t",
                                                            t2=2, g=16)
                                                 [:, tt, :, vi:vi + 113:16],
                                             q_sr.rearrange("p (t2 h t) -> p t2 h t",
                                                            t2=2, h=16)
                                                 [:, tt, :, vi:vi + 113:16],
                                             start=True, stop=True),
                                    "T", 1 if slq == 3 else 0)
                    W("scalar", mmS, last["Pt"])
                    t_exp = bd.do("scalar", lambda E:
                                  E.activation(Pt[:], p3[:], AF.Exp), "S", 1)
                    last["p3"] = t_exp
                    W("tensor", t_exp, last["p2"], *t_vpv)
                    mmPV = None
                    for slq in range(4):
                        vi = 4 * b4 + slq
                        mmPV = bd.do("tensor", lambda E, vi=vi, slq=slq, tt=tt:
                                     E.matmul(p2[:, 65 * slq:65 * slq + 65],
                                              Pt[:, 128 * slq:128 * slq + 128],
                                              v_pv[:, 1040 * tt + 65 * vi:
                                                   1040 * tt + 65 * vi + 65],
                                              start=True, stop=True),
                                     "T", 1 if slq == 3 else 0)
                    last["Pt"] = mmPV
                    W("vector", mmPV, last["o_sb"] if b4 == 0 else None)
                    tr1 = bd.do("vector", lambda E:
                                E.reciprocal(rzb[:], bass.AP(_raw(p2), 64,
                                                             [[512, 128], [65, 4]])),
                                "V", 1)
                    W("vector", tr1)
                    t_norm = bd.do("vector", lambda E, b4=b4:
                                   E.tensor_tensor(
                                       o_sb.rearrange("p (v hd) -> p v hd", hd=64)
                                           [:, 4 * b4:4 * b4 + 4, :],
                                       bass.AP(_raw(p2), 0, [[512, 128], [65, 4], [1, 64]]),
                                       rzb[:].unsqueeze(2).broadcast_to([128, 4, 64]),
                                       ALU.mult), "V", 1)
                    last["p2"] = t_norm
                for b in range(8):
                    W("tensor", t_norm, pv4_last[b % 4])
                    ttr = bd.do("tensor", lambda E, b=b:
                                E.transpose(pv4[b % 4][:, 0:128],
                                            o_sb[:, 128 * b:128 * b + 128], ident[:]),
                                "T", 1)
                    W("scalar", ttr, last["o_mq"] if (tt == 0 and b == 0) else None)
                    teh = bd.do("scalar", lambda E, b=b, tt=tt:
                                E.copy(
                                    o_mq_hi.rearrange("p (m hx) -> p m hx", m=8)[:, b, :]
                                           .rearrange("p (h j) -> p h j", h=16)
                                           [:, :, 8 * tt:8 * tt + 8],
                                    pv4[b % 4][:, 0:128]
                                        .rearrange("p (h j) -> p h j", h=16)), "S", 1)
                    W("vector", teh, last["o_mq"] if (tt == 0 and b == 0) else None)
                    tev = bd.do("vector", lambda E, b=b, tt=tt:
                                E.tensor_sub(
                                    o_mq_lo.rearrange("p (m hx) -> p m hx", m=8)[:, b, :]
                                           .rearrange("p (h j) -> p h j", h=16)
                                           [:, :, 8 * tt:8 * tt + 8],
                                    pv4[b % 4][:, 0:128]
                                        .rearrange("p (h j) -> p h j", h=16),
                                    o_mq_hi.rearrange("p (m hx) -> p m hx", m=8)[:, b, :]
                                           .rearrange("p (h j) -> p h j", h=16)
                                           [:, :, 8 * tt:8 * tt + 8]), "V", 1)
                    pv4_last[b % 4] = tev
                last["o_sb"] = bd.tok("T")
            last["v_pv"] = bd.tok("T")

            if a == NA - 1:
                tatt7 = bd.tok("T")
                lnsf0[0] = ffn_prep_lnonly(0, dxr_f0[0], guards_v=(tatt7,))
                t_xnff_f0[0] = ffn_prep_tr(lnsf0[0])

            t_omq_ev = bd.tok("V")
            for oc in range(8):
                sl = oc % 2
                dwp = load_w(wbig, sl, "B%d" % sl, wproj[oc], sproj[oc], a == 0,
                             t_wbig_prev[sl], sw_att[0])
                W("tensor", dwp, t_omq_ev if oc == 0 else None, last["pmm%d" % (sl % 2)])
                mmp = None
                for m in range(8):
                    mmp = mm_triple(pmm[sl % 2][:, 0:TA],
                                    wsl(wbig, sl, m), wsl(wbig, sl, m, True),
                                    o_mq_hi[:, TA * m:TA * m + TA],
                                    o_mq_lo[:, TA * m:TA * m + TA],
                                    m == 0, m == 7)
                t_wbig_prev[sl] = mmp
                W("vector", mmp)
                tpe = bd.do("vector", lambda E, oc=oc, sl=sl, a=a:
                            E.tensor_scalar(
                                acc.rearrange("p (o r) -> p o r", o=8)[:, oc, :]
                                   .rearrange("p (h g) -> p h g", h=16)
                                   [:, :, 16 * a:16 * a + 16],
                                pmm[sl % 2][:, 0:TA].rearrange("p (h g) -> p h g", h=16),
                                bproj_sb[:, oc:oc + 1], None, ALU.add), "V", 1)
                last["pmm%d" % (sl % 2)] = tpe
            last["o_mq"] = bd.tok("T")
            last["q_sr"] = bd.tok("T")
            if a == 0:
                sw_att[0] = (bd.tok("SW0"), bd.tok("SW1"))
            if a + 1 < NA:
                t_xnf = t_xnf_next

        t_acc_done = bd.tok("V")
        sw_ffn_l = [(bd.tok("SW0"), bd.tok("SW1"))]

        t_s32 = [None, None]
        t_xres_prev = [None, None]
        bd.do("sync", lambda E: E.dma_start(out=selm_sb, in_=selm), "I", 16)
        t_selm = bd.tok("I")

        t_xnff = t_xnff_f0[0]
        for f in range(NF):
            W("tensor", t_xnff, last["p2"], t_gate)
            mlg = None
            for tsub in range(4):
                for kc in range(KC):
                    mlg = mm_triple(p2[:, 8 * tsub:8 * tsub + 8],
                                    xnff_hi[:, TF * kc + 128 * tsub:
                                            TF * kc + 128 * tsub + 128],
                                    xnff_lo[:, TF * kc + 128 * tsub:
                                            TF * kc + 128 * tsub + 128],
                                    gate_hi[:, 8 * kc:8 * kc + 8],
                                    gate_lo[:, 8 * kc:8 * kc + 8],
                                    kc == 0, kc == KC - 1)
            mlg = mlg
            W("vector", mlg)
            tlg = bd.do("vector", lambda E: E.tensor_copy(lgt[:], p2[:, 0:32]), "V", 1)
            last["p2"] = tlg
            W("vector", tlg)
            ts0 = bd.do("vector", lambda E:
                        E.tensor_scalar_mul(rt0[:], lgt[:], -1.0), "V", 1)
            W("scalar", ts0)
            ts1 = bd.do("scalar", lambda E:
                        E.activation(rt0[:], rt0[:], AF.Exp), "S", 1)
            W("vector", ts1)
            bd.do("vector", lambda E: E.tensor_scalar_add(rt0[:], rt0[:], 1.0), "V", 1)
            W("vector", bd.tok("V"))
            tv2 = bd.do("vector", lambda E: E.reciprocal(rt0[:], rt0[:]), "V", 1)
            W("vector", tv2)
            tv3 = bd.do("vector", lambda E:
                        E.tensor_tensor(rt2.rearrange("p (t e) -> p t e", t=4),
                                        rt0.rearrange("p (t e) -> p t e", t=4),
                                        gbias_sb[:].unsqueeze(1).broadcast_to([128, 4, 8]),
                                        ALU.add), "V", 1)
            g3 = rt2.rearrange("p (t g e) -> p t g e", t=4, g=2)
            r8 = rt1.rearrange("p (k x) -> p k x", k=8)

            def rr(i):
                return r8[:, i, :].rearrange("p (t g) -> p t g", t=4)

            def tt_op(dst_off, in0, in1, op):
                return bd.do("vector", lambda E, d=dst_off, a0=in0, a1=in1, op=op:
                             E.tensor_tensor(rr(d) if True else None, a0, a1, op), "V", 1)
            W("vector", tv3)
            tt_op(0, g3[:, :, :, 0], g3[:, :, :, 1], ALU.max)
            tt_op(1, g3[:, :, :, 0], g3[:, :, :, 1], ALU.min)
            tt_op(2, g3[:, :, :, 2], g3[:, :, :, 3], ALU.max)
            td_ = tt_op(3, g3[:, :, :, 2], g3[:, :, :, 3], ALU.min)
            W("vector", td_)
            tt_op(4, rr(0), rr(2), ALU.max)
            tt_op(5, rr(0), rr(2), ALU.min)
            tg_ = tt_op(6, rr(1), rr(3), ALU.max)
            W("vector", tg_)
            th_ = tt_op(5, rr(5), rr(6), ALU.max)
            W("vector", th_)
            ti_ = tt_op(4, rr(4), rr(5), ALU.add)
            gs = rr(4)
            W("vector", ti_)
            tj_ = bd.do("vector", lambda E:
                        E.tensor_tensor(rt1[:, 48:52], gs[:, :, 0], gs[:, :, 1],
                                        ALU.is_ge), "V", 1)
            W("vector", tj_)
            tk_ = bd.do("vector", lambda E:
                        E.tensor_scalar(rt1[:, 52:56], rt1[:, 48:52], -1.0, 1.0,
                                        ALU.mult, ALU.add), "V", 1)
            W("vector", tk_)
            bd.do("vector", lambda E:
                  E.tensor_tensor(rt3.rearrange("p (t g e) -> p t g e", t=4, g=2)
                                  [:, :, 0, :],
                                  g3[:, :, 0, :],
                                  rt1[:, 48:52].unsqueeze(2).broadcast_to([128, 4, 4]),
                                  ALU.mult), "V", 1)
            tm1 = bd.do("vector", lambda E:
                        E.tensor_tensor(rt3.rearrange("p (t g e) -> p t g e", t=4, g=2)
                                        [:, :, 1, :],
                                        g3[:, :, 1, :],
                                        rt1[:, 52:56].unsqueeze(2).broadcast_to([128, 4, 4]),
                                        ALU.mult), "V", 1)
            W("vector", tm1)
            bd.do("vector", lambda E: E.memset(rt1[:, 57:64], -2e30), "V", 1)
            W("vector", bd.tok("V"))
            t_cw = None
            for ts in range(4):
                sl8 = slice(8 * ts, 8 * ts + 8)
                t1_ = bd.do("vector", lambda E, s=sl8:
                            E.reduce_max(rt1[:, 56:57], rt3[:, s], AX.X), "V", 1)
                W("vector", t1_)
                t2_ = bd.do("vector", lambda E, s=sl8:
                            E.match_replace(rt4[:, s], rt1[:, 56:64],
                                            rt3[:, s], -1e30), "V", 1)
                W("vector", t2_)
                t3_ = bd.do("vector", lambda E, s=sl8:
                            E.reduce_max(rt1[:, 56:57], rt4[:, s], AX.X), "V", 1)
                W("vector", t3_)
                t4_ = bd.do("vector", lambda E, s=sl8:
                            E.match_replace(rt5[:, s], rt1[:, 56:64],
                                            rt4[:, s], -1e30), "V", 1)
                W("vector", t4_)
                t5_ = bd.do("vector", lambda E, s=sl8:
                            E.tensor_tensor(rt5[:, s], rt4[:, s], rt5[:, s],
                                            ALU.not_equal), "V", 1)
                W("vector", t5_)
                t6_ = bd.do("vector", lambda E, s=sl8:
                            E.tensor_tensor(rt4[:, s], rt3[:, s], rt4[:, s],
                                            ALU.not_equal), "V", 1)
                W("vector", t6_)
                t7_ = bd.do("vector", lambda E, s=sl8:
                            E.tensor_tensor(rt4[:, s], rt4[:, s], rt5[:, s],
                                            ALU.add), "V", 1)
                W("vector", t7_)
                t8_ = bd.do("vector", lambda E, s=sl8:
                            E.tensor_tensor(rt4[:, s], rt0[:, s], rt4[:, s],
                                            ALU.mult), "V", 1)
                W("vector", t8_)
                t9_ = bd.do("vector", lambda E, s=sl8, ts=ts:
                            E.reduce_sum(rt1[:, 48 + ts:49 + ts], rt4[:, s], AX.X),
                            "V", 1)
                W("vector", t9_)
                ta2 = bd.do("vector", lambda E, ts=ts:
                            E.tensor_scalar_add(rt1[:, 48 + ts:49 + ts],
                                                rt1[:, 48 + ts:49 + ts], 1e-20), "V", 1)
                W("vector", ta2)
                tb2 = bd.do("vector", lambda E, ts=ts:
                            E.reciprocal(rt1[:, 48 + ts:49 + ts],
                                         rt1[:, 48 + ts:49 + ts]), "V", 1)
                W("vector", tb2)
                tc2 = bd.do("vector", lambda E, ts=ts:
                            E.tensor_scalar_mul(rt1[:, 48 + ts:49 + ts],
                                                rt1[:, 48 + ts:49 + ts], 2.5), "V", 1)
                W("vector", tc2)
                t_cw = bd.do("vector", lambda E, s=sl8, ts=ts:
                             E.tensor_scalar_mul(rt2[:, s], rt4[:, s],
                                                 rt1[:, 48 + ts:49 + ts]), "V", 1)
            lnsf_next = ffn_prep_ln(f + 1) if f + 1 < NF else None

            t_rout = [None] * 8
            for ih in range(2):
                t_sh_ev = None
                for ic8 in range(8):
                    ic = 8 * ih + ic8
                    sl = ic8 % 4
                    dws = load_w(wb, sl, "W%d" % sl, wsup[ic], ssup[ic], False,
                                 t_wb_prev[sl], sw_ffn_l[0])
                    W("tensor", dws, last["pmm%d" % (sl % 2)])
                    mms = None
                    for kc in range(KC):
                        mms = mm_triple(pmm[sl % 2][:, 0:TF],
                                        wsl(wb, sl, kc), wsl(wb, sl, kc, True),
                                        xnff_hi[:, TF * kc:TF * kc + TF],
                                        xnff_lo[:, TF * kc:TF * kc + TF],
                                        kc == 0, kc == KC - 1)
                    t_wb_prev[sl] = mms
                    s32 = scr32[:, TF * (ic8 % 2):TF * (ic8 % 2) + TF]
                    W("vector", mms, last["scr"] if ic8 == 0 else None,
                      t_s32[ic8 % 2])
                    tq1 = bd.do("vector", lambda E, sl=sl, s32=s32:
                                E.tensor_scalar_max(s32, pmm[sl % 2][:, 0:TF], 0.0),
                                "V", 1)
                    last["pmm%d" % (sl % 2)] = tq1
                    W("scalar", tq1)
                    tq2 = bd.do("scalar", lambda E, s32=s32:
                                E.square(s32, s32), "S", 1)
                    W("scalar", tq2)
                    tq3 = bd.do("scalar", lambda E, ic8=ic8, s32=s32:
                                E.copy(scr_hi[:, TF * ic8:TF * ic8 + TF], s32), "S", 1)
                    W("vector", tq3)
                    t_sh_ev = bd.do("vector", lambda E, ic8=ic8, s32=s32:
                                    E.tensor_sub(scr_lo[:, TF * ic8:TF * ic8 + TF],
                                                 s32,
                                                 scr_hi[:, TF * ic8:TF * ic8 + TF]),
                                    "V", 1)
                    t_s32[ic8 % 2] = t_sh_ev
                for oc in range(8):
                    sl = oc % 2
                    dwd = load_w(wbig, sl, "B%d" % sl, wsdown[oc, ih],
                                 ssdn[oc, ih], False, t_wbig_prev[sl], sw_ffn_l[0])
                    W("tensor", dwd, t_sh_ev, last["pmm%d" % (sl % 2)])
                    mmd = None
                    for ic8 in range(8):
                        mmd = mm_triple(pmm[sl % 2][:, 0:TF],
                                        wsl(wbig, sl, ic8), wsl(wbig, sl, ic8, True),
                                        scr_hi[:, TF * ic8:TF * ic8 + TF],
                                        scr_lo[:, TF * ic8:TF * ic8 + TF],
                                        ic8 == 0, ic8 == 7)
                    t_wbig_prev[sl] = mmd
                    if ih == 0:
                        W("vector", mmd, last["routed"] if oc == 0 else None)
                        trt = bd.do("vector", lambda E, oc=oc, sl=sl, f=f:
                                    E.tensor_tensor(routed[:, TF * oc:TF * oc + TF],
                                                    pmm[sl % 2][:, 0:TF],
                                                    acc[:, 2048 * oc + TF * f:
                                                        2048 * oc + TF * f + TF],
                                                    ALU.add), "V", 1)
                    else:
                        W("vector", mmd, t_rout[oc])
                        trt = bd.do("vector", lambda E, oc=oc, sl=sl:
                                    E.tensor_tensor(routed[:, TF * oc:TF * oc + TF],
                                                    pmm[sl % 2][:, 0:TF],
                                                    routed[:, TF * oc:TF * oc + TF],
                                                    ALU.add), "V", 1)
                    t_rout[oc] = trt
                    last["pmm%d" % (sl % 2)] = trt
                last["scr"] = bd.tok("T")

            t_cwfm = None
            for ts in range(4):
                W("tensor", t_cw, pv4_last[ts], last["cwfm"] if ts == 0 else None)
                tct = bd.do("tensor", lambda E, ts=ts:
                            E.transpose(pv4[ts][0:8, 0:128], rt2[:, 8 * ts:8 * ts + 8],
                                        ident[:]), "T", 1)
                W("vector", tct)
                t_cwfm = bd.do("vector", lambda E, ts=ts:
                               E.tensor_copy(cwfm[0:8, 128 * ts:128 * ts + 128],
                                             pv4[ts][0:8, 0:128]), "V", 1)
                pv4_last[ts] = t_cwfm

            for ep in range(4):
                t_r2 = [None] * 8
                for e2 in range(2):
                    e = 2 * ep + e2
                    W("tensor", t_cwfm, last["p2"], t_selm)
                    mcw = bd.do("tensor", lambda E, e=e:
                                E.matmul(p2[:, 0:TF], selm_sb[:, 128 * e:128 * e + 128],
                                         cwfm[0:8, :], start=True, stop=True), "T", 1)
                    t_p2u = None
                    for ic in range(4):
                        sl = ic % 4
                        dwu = load_w(wb, sl, "W%d" % sl, wup[e, ic], sup_s[e, ic],
                                     False, t_wb_prev[sl], sw_ffn_l[0])
                        W("tensor", dwu, last["pmm%d" % (sl % 2)])
                        mmu = None
                        for kc in range(KC):
                            mmu = mm_triple(pmm[sl % 2][:, 0:TF],
                                            wsl(wb, sl, kc), wsl(wb, sl, kc, True),
                                            xnff_hi[:, TF * kc:TF * kc + TF],
                                            xnff_lo[:, TF * kc:TF * kc + TF],
                                            kc == 0, kc == KC - 1)
                        t_wb_prev[sl] = mmu
                        ch = 4 * e2 + ic
                        s32 = scr32[:, TF * (ic % 2):TF * (ic % 2) + TF]
                        W("vector", mmu, t_s32[ic % 2],
                          last["scr"] if (e2 == 0 and ic == 0) else None)
                        tq1 = bd.do("vector", lambda E, sl=sl, s32=s32:
                                    E.tensor_scalar_max(s32, pmm[sl % 2][:, 0:TF], 0.0),
                                    "V", 1)
                        last["pmm%d" % (sl % 2)] = tq1
                        W("scalar", tq1)
                        tq2 = bd.do("scalar", lambda E, s32=s32:
                                    E.square(s32, s32), "S", 1)
                        W("vector", tq2, mcw)
                        tq3 = bd.do("vector", lambda E, s32=s32:
                                    E.tensor_tensor(s32, s32, p2[:, 0:TF], ALU.mult),
                                    "V", 1)
                        t_p2u = tq3
                        W("scalar", tq3)
                        tq4 = bd.do("scalar", lambda E, ch=ch, s32=s32:
                                    E.copy(scr_hi[:, TF * ch:TF * ch + TF], s32),
                                    "S", 1)
                        W("vector", tq4)
                        tre = bd.do("vector", lambda E, ch=ch, s32=s32:
                                    E.tensor_sub(scr_lo[:, TF * ch:TF * ch + TF],
                                                 s32,
                                                 scr_hi[:, TF * ch:TF * ch + TF]),
                                    "V", 1)
                        t_r2[ch] = tre
                        t_s32[ic % 2] = tre
                    last["p2"] = t_p2u
                for oc in range(8):
                    sl = oc % 2
                    dwd = load_w(wbig, sl, "B%d" % sl, wdown[oc, ep],
                                 sdn_s[oc, ep], False, t_wbig_prev[sl], sw_ffn_l[0])
                    W("tensor", dwd, last["pmm%d" % (sl % 2)])
                    mmd = None
                    for ch in range(8):
                        if oc == 0:
                            W("tensor", t_r2[ch])
                        mmd = mm_triple(pmm[sl % 2][:, 0:TF],
                                        wsl(wbig, sl, ch), wsl(wbig, sl, ch, True),
                                        scr_hi[:, TF * ch:TF * ch + TF],
                                        scr_lo[:, TF * ch:TF * ch + TF],
                                        ch == 0, ch == 7)
                    t_wbig_prev[sl] = mmd
                    W("vector", mmd, t_rout[oc])
                    trt = bd.do("vector", lambda E, oc=oc, sl=sl:
                                E.tensor_tensor(routed[:, TF * oc:TF * oc + TF],
                                                pmm[sl % 2][:, 0:TF],
                                                routed[:, TF * oc:TF * oc + TF],
                                                ALU.add), "V", 1)
                    t_rout[oc] = trt
                    last["pmm%d" % (sl % 2)] = trt
                last["scr"] = bd.tok("T")
            last["cwfm"] = bd.tok("T")

            if f + 1 < NF:
                last["xnff"] = bd.tok("T")
                t_xnff_next = ffn_prep_tr(lnsf_next)

            p3fin = [None] * 4
            for tsub in range(4):
                osl = tsub % 2
                W("sync", t_xres_prev[osl])
                dxs = bd.do("sync", lambda E, f=f, tsub=tsub, osl=osl:
                            E.dma_start(out=xres[:, 1024 * osl:1024 * osl + 1024],
                                        in_=x_res[512 * f + 128 * tsub:
                                                  512 * f + 128 * tsub + 128, :]),
                            "XR%d" % osl, 16)
                tfe = None
                pfin = [p3, p2]
                pfin_tok = [last["p3"], last["p2"]]
                for oc in range(8):
                    W("tensor", t_rout[oc] if tsub == 0 else None,
                      pfin_tok[oc % 2] if oc < 2 else None, p3fin[oc % 2])
                    tft = bd.do("tensor", lambda E, oc=oc, tsub=tsub:
                                E.transpose(pfin[oc % 2][:, 0:128],
                                            routed[:, TF * oc + 128 * tsub:
                                                   TF * oc + 128 * tsub + 128],
                                            ident[:]), "T", 1)
                    W("vector", tft, t_outp_prev[osl] if oc == 0 else None,
                      dxs if oc == 0 else None)
                    tfe = bd.do("vector", lambda E, oc=oc, osl=osl:
                                E.tensor_tensor(
                                    outp[:, 1024 * osl + 128 * oc:
                                         1024 * osl + 128 * oc + 128],
                                    pfin[oc % 2][:, 0:128],
                                    xres[:, 1024 * osl + 128 * oc:
                                         1024 * osl + 128 * oc + 128],
                                    ALU.add), "V", 1)
                    p3fin[oc % 2] = tfe
                last["p3"] = p3fin[0]
                last["p2"] = p3fin[1]
                t_xres_prev[osl] = tfe
                W("gpsimd", tfe)
                t_outp_prev[osl] = bd.do("gpsimd", lambda E, f=f, tsub=tsub, osl=osl:
                                         E.dma_start(
                                             out=out_d[512 * f + 128 * tsub:
                                                       512 * f + 128 * tsub + 128, :],
                                             in_=outp[:, 1024 * osl:1024 * osl + 1024]),
                                         "O%d" % (osl), 16)
            last["routed"] = bd.tok("T")
            if f + 1 < NF:
                t_xnff = t_xnff_next

        for k in ("I", "X", "W0", "W1", "W2", "W3", "WV0", "WV1", "B0", "B1",
                  "SHQ", "SHK", "PV",
                  "PV2", "O0", "O1", "F0", "F1", "SW0", "SW1", "XR0", "XR1",
                  "G0", "G1"):
            bd.W("gpsimd", (k, bd.cnt[k]))

        @block.sync
        def _(E):
            bd.replay("sync", E, sems)

        @block.gpsimd
        def _(E):
            bd.replay("gpsimd", E, sems)

        @block.tensor
        def _(E):
            bd.replay("tensor", E, sems)

        @block.vector
        def _(E):
            bd.replay("vector", E, sems)

        @block.scalar
        def _(E):
            bd.replay("scalar", E, sems)

    return nc


_NC = None


def _get_nc():
    global _NC
    if _NC is None:
        _NC = build_nc()
    return _NC


def _prep_shared(inputs):
    f32 = np.float32
    w_qkv = np.asarray(inputs["L0_attn_w_qkv"], f32)
    w_proj = np.asarray(inputs["L0_attn_w_proj"], f32)
    b_proj = np.asarray(inputs["L0_attn_b_proj"], f32)
    gate_w = np.asarray(inputs["L0_ffn_gate_weight"], f32)
    gate_b = np.asarray(inputs["L0_ffn_gate_bias"], f32)
    up = np.asarray(inputs["L0_ffn_up_weights"], f32)
    down = np.asarray(inputs["L0_ffn_down_weights"], f32)
    sup = np.asarray(inputs["L0_ffn_shared_up_weight"], f32)
    sdown = np.asarray(inputs["L0_ffn_shared_down_weight"], f32)

    C = np.ascontiguousarray
    sh = {}
    sh["wqk"] = C(w_qkv[:2048].reshape(16, 128, 8, 128).transpose(0, 3, 2, 1)
                  .reshape(16, 128, 1024))
    sh["wv"] = C(w_qkv[2048:].reshape(1024, 8, 128).transpose(1, 2, 0))
    sh["wproj"] = C(w_proj.reshape(8, 128, 8, 128).transpose(0, 3, 2, 1)
                    .reshape(8, 128, 1024))
    sh["wup"] = C(up.reshape(8, 4, 128, 8, 128).transpose(0, 1, 4, 3, 2)
                  .reshape(8, 4, 128, 1024))
    sh["wdown"] = C(down.reshape(4, 2, 8, 128, 4, 128).transpose(2, 0, 5, 1, 4, 3)
                    .reshape(8, 4, 128, 1024))
    sh["wsup"] = C(sup.reshape(16, 128, 8, 128).transpose(0, 3, 2, 1)
                   .reshape(16, 128, 1024))
    sh["wsdown"] = C(sdown.reshape(8, 128, 2, 8, 128).transpose(0, 2, 4, 3, 1)
                     .reshape(8, 2, 128, 1024))
    sh["wgate"] = C(gate_w.T.reshape(8, 128, 8).transpose(1, 0, 2).reshape(128, 64))
    sh["bproj"] = C(b_proj.reshape(8, 128).T)
    sh["gbias"] = C(np.broadcast_to(gate_b, (128, 8)))
    sh["identw"] = np.eye(128, dtype=f32)
    jpat = ((np.arange(16 * TA) % 128) // 16) % 8
    km = np.zeros((8, 16 * TA), f32)
    qm = np.zeros((8, 16 * TA), f32)
    for j0 in range(8):
        km[j0] = (jpat == j0).astype(f32)
        qm[j0] = -MASKC * (jpat != j0).astype(f32)
    sh["kmask"] = km
    sh["qmask"] = qm
    selmat = np.zeros((8, 1024), f32)
    for e in range(8):
        selmat[e, 128 * e:128 * e + 128] = 1.0
    sh["selm"] = selmat
    return sh


def kernel(**inputs):
    x = np.asarray(inputs["L0_attn_x"], np.float32)
    sh = _prep_shared(inputs)
    rr = 512 * np.arange(16)[:, None] + np.arange(128)[None, :]
    in_maps = []
    for c in range(8):
        b, jc = c // 4, c % 4
        m = dict(sh)
        m["x_att"] = np.ascontiguousarray(x[b, 2048 * jc:2048 * jc + 2048])
        m["x_res"] = np.ascontiguousarray(x[b][(rr + 128 * jc).ravel()])
        in_maps.append(m)
    nc = _get_nc()
    res = run_bass_kernel_spmd(nc, in_maps, core_ids=list(range(8)))
    out = np.empty((2, 8192, 1024), np.float32)
    for c in range(8):
        b, jc = c // 4, c % 4
        out[b][(rr + 128 * jc).ravel()] = res.results[c]["out"]
    return out


# version 13
# speedup vs baseline: 1.0068x; 1.0068x over previous
import numpy as np
import concourse.bass as bass
import concourse.mybir as mybir
from concourse.bass_utils import run_bass_kernel_spmd

F32 = mybir.dt.float32
BF16 = mybir.dt.bfloat16
ALU = mybir.AluOpType
AF = mybir.ActivationFunctionType
AX = mybir.AxisListType

D = 1024
H = 16
HD = 64
TC = 2048
NA = 8
TA = 256
NF = 4
TF = 512
KC = 8
MASKC = 200.0
EPS = 1e-5
ENGS = ("sync", "gpsimd", "tensor", "vector", "scalar")


def _raw(t):
    return t.tensor if hasattr(t, "tensor") else t


class Build:

    def __init__(self, semnames):
        self.cnt = {k: 0 for k in semnames}
        self.q = {e: [] for e in ENGS}

    def W(self, eng, *toks):
        for t in toks:
            if t is not None:
                self.q[eng].append(("w", t[0], t[1]))

    def do(self, eng, maker, sem=None, n=1):
        if sem is None or n == 0:
            self.q[eng].append(("i", maker, None, 0))
            return None
        self.cnt[sem] += n
        self.q[eng].append(("i", maker, sem, n))
        return (sem, self.cnt[sem])

    def tok(self, sem):
        return (sem, self.cnt[sem])

    def replay(self, eng, E, sems):
        for item in self.q[eng]:
            if item[0] == "w":
                E.wait_ge(sems[item[1]], item[2])
            else:
                _, maker, sem, n = item
                inst = maker(E)
                if sem is not None:
                    inst.then_inc(sems[sem], n)


def build_nc():
    nc = bass.Bass("TRN2", target_bir_lowering=False, debug=False)

    d_in = {}

    def din(name, shape):
        d_in[name] = nc.dram_tensor(name, list(shape), F32, kind="ExternalInput").ap()
        return d_in[name]

    x_att = din("x_att", [TC, D])
    x_res = din("x_res", [TC, D])
    wqk = din("wqk", [16, 128, 1024])
    wv = din("wv", [8, 128, 1024])
    wproj = din("wproj", [8, 128, 1024])
    wup = din("wup", [8, 4, 128, 1024])
    wdown = din("wdown", [8, 4, 128, 1024])
    wsup = din("wsup", [16, 128, 1024])
    wsdown = din("wsdown", [8, 2, 128, 1024])
    wgate = din("wgate", [128, 64])
    bproj = din("bproj", [128, 8])
    gbias = din("gbias", [128, 8])
    identw = din("identw", [128, 128])
    kmask = din("kmask", [8, 16 * TA])
    qmask = din("qmask", [8, 16 * TA])
    selm = din("selm", [8, 1024])
    out_d = nc.dram_tensor("out", [TC, D], F32, kind="ExternalOutput").ap()

    def scr_dram(name, shape):
        return nc.dram_tensor(name, list(shape), BF16, kind="Internal").ap()
    sqk = scr_dram("sqk", [16, 128, 2048])
    sv = scr_dram("sv", [8, 128, 2048])
    sproj = scr_dram("sproj", [8, 128, 2048])
    sup_s = scr_dram("sup_s", [8, 4, 128, 2048])
    sdn_s = scr_dram("sdn_s", [8, 4, 128, 2048])
    ssup = scr_dram("ssup", [16, 128, 2048])
    ssdn = scr_dram("ssdn", [8, 2, 128, 2048])

    from contextlib import ExitStack
    with ExitStack() as _st:
        acc = _st.enter_context(nc.sbuf_tensor("acc", [128, 8 * 2048], F32))
        ident = _st.enter_context(nc.sbuf_tensor("ident", [128, 128], F32))
        bproj_sb = _st.enter_context(nc.sbuf_tensor("bproj_sb", [128, 8], F32))
        gbias_sb = _st.enter_context(nc.sbuf_tensor("gbias_sb", [128, 8], F32))
        gate_sb = _st.enter_context(nc.sbuf_tensor("gate_sb", [128, 64], F32))
        wb = _st.enter_context(nc.sbuf_tensor("wb", [128, 4 * 2048], BF16))
        wbig = _st.enter_context(nc.sbuf_tensor("wbig", [128, 2 * 2048], BF16))
        gate_hi = _st.enter_context(nc.sbuf_tensor("gate_hi", [128, 64], BF16))
        gate_lo = _st.enter_context(nc.sbuf_tensor("gate_lo", [128, 64], BF16))
        stats = _st.enter_context(nc.sbuf_tensor("stats", [128, 48], F32))
        mv = _st.enter_context(nc.sbuf_tensor("mv", [128, 8], F32))
        rstd = _st.enter_context(nc.sbuf_tensor("rstd", [128, 4], F32))
        rzb = _st.enter_context(nc.sbuf_tensor("rzb", [128, 4], F32))
        ARF = 21280
        AR = _st.enter_context(nc.sbuf_tensor("AR", [128, ARF], F32))
        _off = [0]

        def take(parts, elems):
            o = _off[0]
            _off[0] += elems
            assert _off[0] <= ARF, "arena overflow"
            return AR[0:parts, o:o + elems], o
        ARB = 16384
        ARB_T = _st.enter_context(nc.sbuf_tensor("ARB", [128, ARB], BF16))
        _off16 = [0]

        def take16(parts, elems_bf16):
            o = _off16[0]
            _off16[0] += elems_bf16
            assert _off16[0] <= ARB, "bf16 arena overflow"
            return ARB_T[0:parts, o:o + elems_bf16], o
        xa, _ = take(128, 2 * 1024)
        oddst, _ = take(128, 8 * TA)
        wsf, _ = take(128, 2 * 1024)
        xnf_hi, _ = take16(128, 8 * TA)
        xnf_lo, _ = take16(128, 8 * TA)
        wvb, _ = take16(128, 2 * 2048)
        q_sr, _ = take(72, 16 * TA)
        k_sr, _ = take(72, 16 * TA)
        v_tok, c_vtok = take(128, 2 * 1024)
        v_pv, c_vpv = take(128, 2 * 16 * 65)
        Pt, _ = take(128, 512)
        o_sb, _ = take(128, 1024)
        o_mq_hi, _ = take16(128, 8 * TA)
        o_mq_lo, _ = take16(128, 8 * TA)
        spl, _ = take16(128, 2 * 2048)
        _off[0] = 0
        _off16[0] = 0
        xr, _ = take(128, 4 * 1024)
        wsf_f, _ = take(128, 2 * 1024)
        xnr, _ = take(128, 4 * 1024)
        xnff_hi, _ = take16(128, 8 * TF)
        xnff_lo, _ = take16(128, 8 * TF)
        lgt, _ = take(128, 32)
        rt0, _ = take(128, 32)
        rt1, _ = take(128, 64)
        rt2, _ = take(128, 32)
        rt3, _ = take(128, 32)
        rt4, _ = take(128, 32)
        rt5, _ = take(128, 32)
        cwfm, _ = take(128, TF)
        selm_sb, _ = take(8, 1024)
        scr_hi, _ = take16(128, 8 * TF)
        scr_lo, _ = take16(128, 8 * TF)
        scr32, _ = take(128, 2 * TF)
        routed, _ = take(128, 8 * TF)
        xres, _ = take(128, 2 * 1024)
        outp, _ = take(128, 2 * 1024)
        p0 = _st.enter_context(nc.psum_tensor("p0", [128, 512], F32))
        p1 = _st.enter_context(nc.psum_tensor("p1", [128, 512], F32))
        p2 = _st.enter_context(nc.psum_tensor("p2", [128, 512], F32))
        p3 = _st.enter_context(nc.psum_tensor("p3", [128, 512], F32))
        p4 = _st.enter_context(nc.psum_tensor("p4", [128, 512], F32))
        p5 = _st.enter_context(nc.psum_tensor("p5", [128, 512], F32))
        p6 = _st.enter_context(nc.psum_tensor("p6", [128, 512], F32))
        p7 = _st.enter_context(nc.psum_tensor("p7", [128, 512], F32))
        semnames = ["I", "X", "W0", "W1", "W2", "W3", "WV0", "WV1", "B0", "B1",
                    "SHQ", "SHK", "PV", "PV2", "O0", "O1",
                    "F0", "F1", "SW0", "SW1",
                    "XR0", "XR1", "G0", "G1", "T", "V", "S", "P"]
        sems = {k: _st.enter_context(nc.semaphore("s" + k)) for k in semnames}
        block = _st.enter_context(nc.Block())
        bd = Build(list(sems))
        W = bd.W
        pmm = [p0, p1]
        pv4 = [p4, p5, p6, p7]
        pv4_last = [None] * 4

        for ap_src, dst in ((identw, ident), (bproj, bproj_sb), (gbias, gbias_sb),
                            (wgate, gate_sb)):
            bd.do("sync", lambda E, s=ap_src, d=dst: E.dma_start(out=d[:], in_=s), "I", 16)
        bd.do("sync", lambda E: E.dma_start(out=k_sr[64:72, :], in_=kmask), "I", 16)
        bd.do("sync", lambda E: E.dma_start(out=q_sr[64:72, :], in_=qmask), "I", 16)
        t_init = bd.tok("I")
        W("tensor", t_init)
        W("vector", t_init)
        W("scalar", t_init)
        v_ones = bd.do("vector", lambda E: E.memset(v_pv[:], 1.0), "V", 1)
        tg1 = bd.do("scalar", lambda E: E.copy(gate_hi[:], gate_sb[:]), "S", 1)
        W("vector", tg1)
        t_gate = bd.do("vector", lambda E:
                       E.tensor_sub(gate_lo[:], gate_sb[:], gate_hi[:]), "V", 1)

        last = {
            "xa": None, "xnf": None, "q_sr": None, "k_sr": None, "oddst": None,
            "v_tok": None, "v_pv": None, "Pt": None, "o_sb": None, "o_mq": None,
            "pmm0": None, "pmm1": None, "p2": None, "p3": None,
            "xr": None, "xnr0": None, "xnff": None,
            "scr": None, "routed": None, "cwfm": None,
        }
        t_wb_prev = [None] * 4
        t_wv_prev = [None, None]
        t_wbig_prev = [None, None]
        t_outp_prev = [None, None]
        t_wsf_prev = [None] * 4
        t_wb_scrw = [None, None]
        t_spl_scrw = [None, None]

        def load_w(buf, sl, wsem, fp32_src, scratch_blk, first, reuse_tok,
                   sw_toks=None):
            if first:
                fs = sl % 2
                W("sync", t_wsf_prev[fs])
                dfi = bd.do("sync", lambda E, fs=fs, src=fp32_src:
                            E.dma_start(out=wsf[:, 1024 * fs:1024 * fs + 1024],
                                        in_=src), "F%d" % fs, 16)
                W("scalar", dfi, t_wb_scrw[fs], reuse_tok)
                thi = bd.do("scalar", lambda E, fs=fs, sl=sl, b=buf:
                            E.copy(b[:, 2048 * sl:2048 * sl + 1024],
                                   wsf[:, 1024 * fs:1024 * fs + 1024]), "S", 1)
                W("vector", thi)
                tlo = bd.do("vector", lambda E, fs=fs, sl=sl, b=buf:
                            E.tensor_sub(b[:, 2048 * sl + 1024:2048 * sl + 2048],
                                         wsf[:, 1024 * fs:1024 * fs + 1024],
                                         b[:, 2048 * sl:2048 * sl + 1024]), "V", 1)
                t_wsf_prev[fs] = tlo
                W("gpsimd", tlo)
                tsw = bd.do("gpsimd", lambda E, sl=sl, b=buf, dst=scratch_blk:
                            E.dma_start(out=dst, in_=b[:, 2048 * sl:2048 * sl + 2048]),
                            "SW%d" % (sl % 2), 16)
                t_wb_scrw[fs] = tsw
                return tlo
            else:
                W("sync", reuse_tok, *(sw_toks or ()))
                dw = bd.do("sync", lambda E, sl=sl, b=buf, src=scratch_blk:
                           E.dma_start(out=b[:, 2048 * sl:2048 * sl + 2048],
                                       in_=src), wsem, 16)
                return dw

        def presplit(fp32_src, scratch_blk, pi):
            fs = pi % 2
            ps = pi % 2
            W("sync", t_wsf_prev[fs])
            dfi = bd.do("sync", lambda E, fs=fs, src=fp32_src:
                        E.dma_start(out=wsf[:, 1024 * fs:1024 * fs + 1024],
                                    in_=src), "F%d" % fs, 16)
            W("scalar", dfi, t_spl_scrw[ps])
            thi = bd.do("scalar", lambda E, fs=fs, ps=ps:
                        E.copy(spl[:, 2048 * ps:2048 * ps + 1024],
                               wsf[:, 1024 * fs:1024 * fs + 1024]), "S", 1)
            W("vector", thi)
            tlo = bd.do("vector", lambda E, fs=fs, ps=ps:
                        E.tensor_sub(spl[:, 2048 * ps + 1024:2048 * ps + 2048],
                                     wsf[:, 1024 * fs:1024 * fs + 1024],
                                     spl[:, 2048 * ps:2048 * ps + 1024]), "P", 1)
            t_wsf_prev[fs] = tlo
            W("gpsimd", tlo)
            t_spl_scrw[ps] = bd.do("gpsimd", lambda E, ps=ps, dst=scratch_blk:
                                   E.dma_start(out=dst,
                                               in_=spl[:, 2048 * ps:2048 * ps + 2048]),
                                   "SW%d" % ps, 16)

        def wsl(buf, sl, kc, lo=False):
            o = 2048 * sl + (1024 if lo else 0) + 128 * kc
            return buf[:, o:o + 128]

        def mm_triple(psum_ap, a_hi, a_lo, b_hi, b_lo, first_flag, stop_flag,
                      inc=None):
            if inc is None:
                inc = stop_flag
            ops = [(a_hi, b_hi), (a_hi, b_lo), (a_lo, b_hi)]
            mmi = None
            for i, (wa, xa_) in enumerate(ops):
                mmi = bd.do("tensor", lambda E, wa=wa, xa_=xa_, i=i:
                            E.matmul(psum_ap, wa, xa_,
                                     start=(first_flag and i == 0),
                                     stop=(stop_flag and i == 2)),
                            "T", 1 if (inc and i == 2) else 0)
            return mmi

        def ln_subtile(src_sb, src_off, dst_sb, dst_off, st_i, v_deps):
            W("vector", *v_deps)
            bd.do("vector", lambda E, o=src_off, i=st_i:
                  E.bn_stats(stats[:, 12 * i:12 * i + 6], src_sb[:, o:o + 512]), "V", 1)
            t2 = bd.do("vector", lambda E, o=src_off, i=st_i:
                       E.bn_stats(stats[:, 12 * i + 6:12 * i + 12],
                                  src_sb[:, o + 512:o + 1024]), "V", 1)
            W("vector", t2)
            t3 = bd.do("vector", lambda E, i=st_i:
                       E.bn_aggr(mv[:, 2 * i:2 * i + 2], stats[:, 12 * i:12 * i + 12]),
                       "V", 1)
            W("vector", t3)
            t3b = bd.do("vector", lambda E, i=st_i:
                        E.tensor_scalar_add(rstd[:, i:i + 1],
                                            mv[:, 2 * i + 1:2 * i + 2], EPS), "V", 1)
            W("scalar", t3b)
            t4 = bd.do("scalar", lambda E, i=st_i:
                       E.activation(rstd[:, i:i + 1], rstd[:, i:i + 1], AF.Ln), "S", 1)
            W("vector", t4)
            t4b = bd.do("vector", lambda E, i=st_i:
                        E.tensor_scalar_mul(rstd[:, i:i + 1], rstd[:, i:i + 1], -0.5),
                        "V", 1)
            W("scalar", t4b)
            t5 = bd.do("scalar", lambda E, i=st_i:
                       E.activation(rstd[:, i:i + 1], rstd[:, i:i + 1], AF.Exp), "S", 1)
            W("vector", t5)
            t6 = bd.do("vector", lambda E, so=src_off, do_=dst_off, i=st_i,
                       s=src_sb, d=dst_sb:
                       E.tensor_scalar(d[:, do_:do_ + 1024], s[:, so:so + 1024],
                                       mv[:, 2 * i:2 * i + 1], rstd[:, i:i + 1],
                                       ALU.subtract, ALU.mult), "V", 1)
            return t6

        def transpose_subtile(src_sb, src_off, dst_hi, dst_lo, FT, col_off,
                              v_dep, reuse_tok):
            ev = None
            for c in range(8):
                W("tensor", v_dep, pv4_last[c % 4])
                tt = bd.do("tensor", lambda E, c=c, o=src_off, s=src_sb:
                           E.transpose(pv4[c % 4][:, 0:128],
                                       s[:, o + 128 * c:o + 128 * c + 128],
                                       ident[:]), "T", 1)
                W("scalar", tt, reuse_tok if c == 0 else None)
                eh = bd.do("scalar", lambda E, c=c, co=col_off, d=dst_hi, FT=FT:
                           E.copy(d[:, c * FT + co:c * FT + co + 128],
                                  pv4[c % 4][:, 0:128]), "S", 1)
                W("vector", eh, reuse_tok if c == 0 else None)
                ev = bd.do("vector", lambda E, c=c, co=col_off, dh=dst_hi, dl=dst_lo,
                           FT=FT:
                           E.tensor_sub(dl[:, c * FT + co:c * FT + co + 128],
                                        pv4[c % 4][:, 0:128],
                                        dh[:, c * FT + co:c * FT + co + 128]), "V", 1)
                pv4_last[c % 4] = ev
            return ev

        def att_prep_ln(a):
            W("sync", last["xa"])
            dx = None
            for ss in range(2):
                dx = bd.do("sync", lambda E, ss=ss, a=a:
                           E.dma_start(out=xa[:, 1024 * ss:1024 * ss + 1024],
                                       in_=x_att[256 * a + 128 * ss:
                                                 256 * a + 128 * ss + 128, :]),
                           "X", 16)
            lns = []
            for ss in range(2):
                lns.append(ln_subtile(xa, 1024 * ss, xa, 1024 * ss, ss,
                                      [dx, last["xa"]]))
            return lns

        def att_prep_tr(lns):
            t_xnf = None
            for ss in range(2):
                t_xnf = transpose_subtile(xa, 1024 * ss, xnf_hi, xnf_lo, TA,
                                          128 * ss, lns[ss],
                                          last["xnf"] if ss == 0 else None)
            last["xa"] = bd.tok("T")
            return t_xnf

        def ffn_prep_dma(f, guards_sync=()):
            W("sync", last["xr"], *guards_sync)
            dxr = None
            for tsub in range(4):
                dxr = bd.do("sync", lambda E, f=f, tsub=tsub:
                            E.dma_start(out=xr[:, 1024 * tsub:1024 * tsub + 1024],
                                        in_=x_res[512 * f + 128 * tsub:
                                                  512 * f + 128 * tsub + 128, :]),
                            "X", 16)
            return dxr

        def ffn_prep_lnonly(f, dxr, guards_v=()):
            lns = []
            for tsub in range(4):
                lns.append(ln_subtile(xr, 1024 * tsub, xnr, 1024 * tsub, tsub,
                                      [dxr, last["xnr0"]] + list(guards_v)))
            last["xr"] = lns[3]
            return lns

        def ffn_prep_ln(f, guards_sync=(), guards_v=()):
            dxr = ffn_prep_dma(f, guards_sync)
            return ffn_prep_lnonly(f, dxr, guards_v)

        def ffn_prep_tr(lns):
            t_x = None
            for tsub in range(4):
                t_x = transpose_subtile(xnr, 1024 * tsub, xnff_hi, xnff_lo, TF,
                                        128 * tsub, lns[tsub],
                                        last["xnff"] if tsub == 0 else None)
            last["xnr0"] = bd.tok("T")
            return t_x

        presplit_jobs = []
        for ic in range(16):
            presplit_jobs.append((wsup[ic], ssup[ic]))
        for e in range(8):
            for ic in range(4):
                presplit_jobs.append((wup[e, ic], sup_s[e, ic]))
        for oc in range(8):
            for ep in range(4):
                presplit_jobs.append((wdown[oc, ep], sdn_s[oc, ep]))
        for oc in range(8):
            for ih in range(2):
                presplit_jobs.append((wsdown[oc, ih], ssdn[oc, ih]))
        assert len(presplit_jobs) == 96

        lns0 = att_prep_ln(0)
        t_xnf = att_prep_tr(lns0)
        sw_att = [None]
        lnsf0 = [None]
        dxr_f0 = [None]
        t_xnff_f0 = [None]
        for a in range(NA):
            t_vmm = None
            for kc in range(KC):
                sl = kc % 2
                dwv = load_w(wvb, sl, "WV%d" % sl, wv[kc], sv[kc], a == 0,
                             t_wv_prev[sl], sw_att[0])
                W("tensor", dwv)
                if kc == 0:
                    W("tensor", *pv4_last)
                    W("tensor", last["v_tok"])
                for tsub in range(2):
                    for vc in range(2):
                        t_vmm = mm_triple(
                            pv4[tsub * 2 + vc][:, 0:512],
                            xnf_hi[:, TA * kc + 128 * tsub:TA * kc + 128 * tsub + 128],
                            xnf_lo[:, TA * kc + 128 * tsub:TA * kc + 128 * tsub + 128],
                            wvb[:, 2048 * sl + 512 * vc:2048 * sl + 512 * vc + 512],
                            wvb[:, 2048 * sl + 1024 + 512 * vc:
                                2048 * sl + 1024 + 512 * vc + 512],
                            kc == 0, kc == KC - 1,
                            inc=(tsub == 1 and vc == 1))
                t_wv_prev[sl] = t_vmm
            W("scalar", t_vmm, last["v_tok"])
            for tsub in range(2):
                for vc in range(2):
                    tv = bd.do("scalar", lambda E, tsub=tsub, vc=vc:
                               E.copy(v_tok[:, 1024 * tsub + 512 * vc:
                                             1024 * tsub + 512 * vc + 512],
                                      pv4[tsub * 2 + vc][:, 0:512]), "S", 1)
                    pv4_last[tsub * 2 + vc] = tv
            t_vev = tv
            lns_next = att_prep_ln(a + 1) if a + 1 < NA else None
            W("scalar", t_vev, v_ones, last["v_pv"])
            t_vpv = {}
            for g in range(16):
                for tsub in range(2):
                    eng, sem = (("scalar", "PV") if (2 * g + tsub) % 2 == 0
                                else ("scalar", "PV2"))
                    t_vpv[sem] = bd.do(eng, lambda E, g=g, tsub=tsub:
                                       E.dma_start(
                                           out=bass.AP(_raw(AR),
                                                       8 * g * ARF + c_vpv + 1040 * tsub,
                                                       [[ARF, 8], [65, 16], [1, 64]]),
                                           in_=bass.AP(_raw(AR),
                                                       g * 64 + c_vtok + 1024 * tsub,
                                                       [[ARF, 128], [1, 64]])),
                                       sem, 16)
            t_vpv = (t_vpv["PV"], t_vpv["PV2"])
            last["v_tok"] = t_vpv[0]

            t_shift = {"q": None, "k": None}
            t_evac_qk = None
            for oc in range(16):
                sl = oc % 4
                dw = load_w(wb, sl, "W%d" % sl, wqk[oc], sqk[oc], a == 0,
                            t_wb_prev[sl], sw_att[0])
                W("tensor", dw, t_xnf, last["pmm%d" % (sl % 2)])
                mm = None
                for kc in range(KC):
                    mm = mm_triple(pmm[sl % 2][:, 0:TA],
                                   wsl(wb, sl, kc), wsl(wb, sl, kc, True),
                                   xnf_hi[:, TA * kc:TA * kc + TA],
                                   xnf_lo[:, TA * kc:TA * kc + TA],
                                   kc == 0, kc == KC - 1)
                t_wb_prev[sl] = mm
                dst = q_sr if oc < 8 else k_sr
                c = oc % 8
                W("scalar", mm,
                  last["q_sr"] if oc == 0 else None,
                  t_shift["q"] if oc == 8 else None,
                  last["oddst"] if oc == 0 else None)
                if oc < 8:
                    W("vector", mm,
                      last["q_sr"] if oc == 0 else None,
                      last["oddst"] if oc == 0 else None)
                    e1 = bd.do("vector", lambda E, c=c, sl=sl, dst=dst:
                               E.tensor_scalar_mul(
                                   dst.rearrange("p (t2 h t) -> p t2 h t", t2=2, h=16)
                                      [0:64, :, 2 * c, :],
                                   pmm[sl % 2][0:64, 0:TA].rearrange("p (t2 t) -> p t2 t", t2=2),
                                   0.125), "V", 1)
                    e2 = bd.do("vector", lambda E, c=c, sl=sl:
                               E.tensor_scalar_mul(
                                   oddst.rearrange("p (t2 c t) -> p t2 c t", t2=2, c=8)
                                        [64:128, :, c, :],
                                   pmm[sl % 2][64:128, 0:TA].rearrange("p (t2 t) -> p t2 t", t2=2),
                                   0.125), "V", 1)
                else:
                    e1 = bd.do("scalar", lambda E, c=c, sl=sl, dst=dst:
                               E.copy(dst.rearrange("p (t2 h t) -> p t2 h t", t2=2, h=16)
                                         [0:64, :, 2 * c, :],
                                      pmm[sl % 2][0:64, 0:TA].rearrange("p (t2 t) -> p t2 t", t2=2)),
                               "S", 1)
                    e2 = bd.do("scalar", lambda E, c=c, sl=sl:
                               E.copy(oddst.rearrange("p (t2 c t) -> p t2 c t", t2=2, c=8)
                                           [64:128, :, c, :],
                                      pmm[sl % 2][64:128, 0:TA].rearrange("p (t2 t) -> p t2 t", t2=2)),
                               "S", 1)
                last["pmm%d" % (sl % 2)] = e2
                t_evac_qk = e2
                if a >= 2:
                    pj = 16 * (a - 2) + oc
                    presplit(presplit_jobs[pj][0], presplit_jobs[pj][1], pj)
                if oc in (7, 15):
                    key = "q" if oc == 7 else "k"
                    eng, shsem = ("sync", "SHQ") if oc == 7 else ("scalar", "SHK")
                    W(eng, e2, last["q_sr"])
                    t_shift[key] = bd.do(eng, lambda E, dst=dst:
                                         E.dma_start(
                                             out=dst.rearrange("p (t2 h t) -> p t2 h t",
                                                               t2=2, h=16)[0:64, :, 1::2, :],
                                             in_=oddst[64:128, :]), shsem, 16)
            last["oddst"] = t_shift["k"]

            if a + 1 < NA:
                last["xnf"] = bd.tok("T")
                t_xnf_next = att_prep_tr(lns_next)
            if a == NA - 1:
                tqk7 = bd.tok("T")
                dxr_f0[0] = ffn_prep_dma(0, guards_sync=(tqk7, t_shift["q"],
                                                         t_shift["k"]))

            for tt in range(2):
                for b4 in range(4):
                    W("tensor", t_shift["q"], t_shift["k"], t_evac_qk,
                      last["p3"])
                    mmS = None
                    for slq in range(4):
                        vi = 4 * b4 + slq
                        mmS = bd.do("tensor", lambda E, vi=vi, slq=slq, tt=tt:
                                    E.matmul(p3[:, 128 * slq:128 * slq + 128],
                                             k_sr.rearrange("p (t2 g t) -> p t2 g t",
                                                            t2=2, g=16)
                                                 [:, tt, :, vi:vi + 113:16],
                                             q_sr.rearrange("p (t2 h t) -> p t2 h t",
                                                            t2=2, h=16)
                                                 [:, tt, :, vi:vi + 113:16],
                                             start=True, stop=True),
                                    "T", 1 if slq == 3 else 0)
                    W("scalar", mmS, last["Pt"])
                    t_exp = bd.do("scalar", lambda E:
                                  E.activation(Pt[:], p3[:], AF.Exp), "S", 1)
                    last["p3"] = t_exp
                    W("tensor", t_exp, last["p2"], *t_vpv)
                    mmPV = None
                    for slq in range(4):
                        vi = 4 * b4 + slq
                        mmPV = bd.do("tensor", lambda E, vi=vi, slq=slq, tt=tt:
                                     E.matmul(p2[:, 65 * slq:65 * slq + 65],
                                              Pt[:, 128 * slq:128 * slq + 128],
                                              v_pv[:, 1040 * tt + 65 * vi:
                                                   1040 * tt + 65 * vi + 65],
                                              start=True, stop=True),
                                     "T", 1 if slq == 3 else 0)
                    last["Pt"] = mmPV
                    W("vector", mmPV, last["o_sb"] if b4 == 0 else None)
                    tr1 = bd.do("vector", lambda E:
                                E.reciprocal(rzb[:], bass.AP(_raw(p2), 64,
                                                             [[512, 128], [65, 4]])),
                                "V", 1)
                    W("vector", tr1)
                    t_norm = bd.do("vector", lambda E, b4=b4:
                                   E.tensor_tensor(
                                       o_sb.rearrange("p (v hd) -> p v hd", hd=64)
                                           [:, 4 * b4:4 * b4 + 4, :],
                                       bass.AP(_raw(p2), 0, [[512, 128], [65, 4], [1, 64]]),
                                       rzb[:].unsqueeze(2).broadcast_to([128, 4, 64]),
                                       ALU.mult), "V", 1)
                    last["p2"] = t_norm
                for b in range(8):
                    W("tensor", t_norm, pv4_last[b % 4])
                    ttr = bd.do("tensor", lambda E, b=b:
                                E.transpose(pv4[b % 4][:, 0:128],
                                            o_sb[:, 128 * b:128 * b + 128], ident[:]),
                                "T", 1)
                    W("scalar", ttr, last["o_mq"] if (tt == 0 and b == 0) else None)
                    teh = bd.do("scalar", lambda E, b=b, tt=tt:
                                E.copy(
                                    o_mq_hi.rearrange("p (m hx) -> p m hx", m=8)[:, b, :]
                                           .rearrange("p (h j) -> p h j", h=16)
                                           [:, :, 8 * tt:8 * tt + 8],
                                    pv4[b % 4][:, 0:128]
                                        .rearrange("p (h j) -> p h j", h=16)), "S", 1)
                    W("vector", teh, last["o_mq"] if (tt == 0 and b == 0) else None)
                    tev = bd.do("vector", lambda E, b=b, tt=tt:
                                E.tensor_sub(
                                    o_mq_lo.rearrange("p (m hx) -> p m hx", m=8)[:, b, :]
                                           .rearrange("p (h j) -> p h j", h=16)
                                           [:, :, 8 * tt:8 * tt + 8],
                                    pv4[b % 4][:, 0:128]
                                        .rearrange("p (h j) -> p h j", h=16),
                                    o_mq_hi.rearrange("p (m hx) -> p m hx", m=8)[:, b, :]
                                           .rearrange("p (h j) -> p h j", h=16)
                                           [:, :, 8 * tt:8 * tt + 8]), "V", 1)
                    pv4_last[b % 4] = tev
                last["o_sb"] = bd.tok("T")
            last["v_pv"] = bd.tok("T")

            if a == NA - 1:
                tatt7 = bd.tok("T")
                lnsf0[0] = ffn_prep_lnonly(0, dxr_f0[0], guards_v=(tatt7,))
                t_xnff_f0[0] = ffn_prep_tr(lnsf0[0])

            t_omq_ev = bd.tok("V")
            for oc in range(8):
                sl = oc % 2
                dwp = load_w(wbig, sl, "B%d" % sl, wproj[oc], sproj[oc], a == 0,
                             t_wbig_prev[sl], sw_att[0])
                W("tensor", dwp, t_omq_ev if oc == 0 else None, last["pmm%d" % (sl % 2)])
                mmp = None
                for m in range(8):
                    mmp = mm_triple(pmm[sl % 2][:, 0:TA],
                                    wsl(wbig, sl, m), wsl(wbig, sl, m, True),
                                    o_mq_hi[:, TA * m:TA * m + TA],
                                    o_mq_lo[:, TA * m:TA * m + TA],
                                    m == 0, m == 7)
                t_wbig_prev[sl] = mmp
                W("vector", mmp)
                tpe = bd.do("vector", lambda E, oc=oc, sl=sl, a=a:
                            E.tensor_scalar(
                                acc.rearrange("p (o r) -> p o r", o=8)[:, oc, :]
                                   .rearrange("p (h g) -> p h g", h=16)
                                   [:, :, 16 * a:16 * a + 16],
                                pmm[sl % 2][:, 0:TA].rearrange("p (h g) -> p h g", h=16),
                                bproj_sb[:, oc:oc + 1], None, ALU.add), "V", 1)
                last["pmm%d" % (sl % 2)] = tpe
            last["o_mq"] = bd.tok("T")
            last["q_sr"] = bd.tok("T")
            if a == 0:
                sw_att[0] = (bd.tok("SW0"), bd.tok("SW1"))
            if a + 1 < NA:
                t_xnf = t_xnf_next

        t_acc_done = bd.tok("V")
        sw_ffn_l = [(bd.tok("SW0"), bd.tok("SW1"))]

        t_s32 = [None, None]
        t_xres_prev = [None, None]
        bd.do("sync", lambda E: E.dma_start(out=selm_sb, in_=selm), "I", 16)
        t_selm = bd.tok("I")

        t_xnff = t_xnff_f0[0]
        for f in range(NF):
            W("tensor", t_xnff, last["p2"], t_gate)
            mlg = None
            for tsub in range(4):
                for kc in range(KC):
                    mlg = mm_triple(p2[:, 8 * tsub:8 * tsub + 8],
                                    xnff_hi[:, TF * kc + 128 * tsub:
                                            TF * kc + 128 * tsub + 128],
                                    xnff_lo[:, TF * kc + 128 * tsub:
                                            TF * kc + 128 * tsub + 128],
                                    gate_hi[:, 8 * kc:8 * kc + 8],
                                    gate_lo[:, 8 * kc:8 * kc + 8],
                                    kc == 0, kc == KC - 1)
            mlg = mlg
            W("vector", mlg)
            tlg = bd.do("vector", lambda E: E.tensor_copy(lgt[:], p2[:, 0:32]), "V", 1)
            last["p2"] = tlg
            W("vector", tlg)
            ts0 = bd.do("vector", lambda E:
                        E.tensor_scalar_mul(rt0[:], lgt[:], -1.0), "V", 1)
            W("scalar", ts0)
            ts1 = bd.do("scalar", lambda E:
                        E.activation(rt0[:], rt0[:], AF.Exp), "S", 1)
            W("vector", ts1)
            bd.do("vector", lambda E: E.tensor_scalar_add(rt0[:], rt0[:], 1.0), "V", 1)
            W("vector", bd.tok("V"))
            tv2 = bd.do("vector", lambda E: E.reciprocal(rt0[:], rt0[:]), "V", 1)
            W("vector", tv2)
            tv3 = bd.do("vector", lambda E:
                        E.tensor_tensor(rt2.rearrange("p (t e) -> p t e", t=4),
                                        rt0.rearrange("p (t e) -> p t e", t=4),
                                        gbias_sb[:].unsqueeze(1).broadcast_to([128, 4, 8]),
                                        ALU.add), "V", 1)
            g3 = rt2.rearrange("p (t g e) -> p t g e", t=4, g=2)
            r8 = rt1.rearrange("p (k x) -> p k x", k=8)

            def rr(i):
                return r8[:, i, :].rearrange("p (t g) -> p t g", t=4)

            def tt_op(dst_off, in0, in1, op):
                return bd.do("vector", lambda E, d=dst_off, a0=in0, a1=in1, op=op:
                             E.tensor_tensor(rr(d) if True else None, a0, a1, op), "V", 1)
            W("vector", tv3)
            tt_op(0, g3[:, :, :, 0], g3[:, :, :, 1], ALU.max)
            tt_op(1, g3[:, :, :, 0], g3[:, :, :, 1], ALU.min)
            tt_op(2, g3[:, :, :, 2], g3[:, :, :, 3], ALU.max)
            td_ = tt_op(3, g3[:, :, :, 2], g3[:, :, :, 3], ALU.min)
            W("vector", td_)
            tt_op(4, rr(0), rr(2), ALU.max)
            tt_op(5, rr(0), rr(2), ALU.min)
            tg_ = tt_op(6, rr(1), rr(3), ALU.max)
            W("vector", tg_)
            th_ = tt_op(5, rr(5), rr(6), ALU.max)
            W("vector", th_)
            ti_ = tt_op(4, rr(4), rr(5), ALU.add)
            gs = rr(4)
            W("vector", ti_)
            tj_ = bd.do("vector", lambda E:
                        E.tensor_tensor(rt1[:, 48:52], gs[:, :, 0], gs[:, :, 1],
                                        ALU.is_ge), "V", 1)
            W("vector", tj_)
            tk_ = bd.do("vector", lambda E:
                        E.tensor_scalar(rt1[:, 52:56], rt1[:, 48:52], -1.0, 1.0,
                                        ALU.mult, ALU.add), "V", 1)
            W("vector", tk_)
            bd.do("vector", lambda E:
                  E.tensor_tensor(rt3.rearrange("p (t g e) -> p t g e", t=4, g=2)
                                  [:, :, 0, :],
                                  g3[:, :, 0, :],
                                  rt1[:, 48:52].unsqueeze(2).broadcast_to([128, 4, 4]),
                                  ALU.mult), "V", 1)
            tm1 = bd.do("vector", lambda E:
                        E.tensor_tensor(rt3.rearrange("p (t g e) -> p t g e", t=4, g=2)
                                        [:, :, 1, :],
                                        g3[:, :, 1, :],
                                        rt1[:, 52:56].unsqueeze(2).broadcast_to([128, 4, 4]),
                                        ALU.mult), "V", 1)
            W("vector", tm1)
            bd.do("vector", lambda E: E.memset(rt1[:, 57:64], -2e30), "V", 1)
            W("vector", bd.tok("V"))
            t_cw = None
            for ts in range(4):
                sl8 = slice(8 * ts, 8 * ts + 8)
                t1_ = bd.do("vector", lambda E, s=sl8:
                            E.reduce_max(rt1[:, 56:57], rt3[:, s], AX.X), "V", 1)
                W("vector", t1_)
                t2_ = bd.do("vector", lambda E, s=sl8:
                            E.match_replace(rt4[:, s], rt1[:, 56:64],
                                            rt3[:, s], -1e30), "V", 1)
                W("vector", t2_)
                t3_ = bd.do("vector", lambda E, s=sl8:
                            E.reduce_max(rt1[:, 56:57], rt4[:, s], AX.X), "V", 1)
                W("vector", t3_)
                t4_ = bd.do("vector", lambda E, s=sl8:
                            E.match_replace(rt5[:, s], rt1[:, 56:64],
                                            rt4[:, s], -1e30), "V", 1)
                W("vector", t4_)
                t5_ = bd.do("vector", lambda E, s=sl8:
                            E.tensor_tensor(rt5[:, s], rt4[:, s], rt5[:, s],
                                            ALU.not_equal), "V", 1)
                W("vector", t5_)
                t6_ = bd.do("vector", lambda E, s=sl8:
                            E.tensor_tensor(rt4[:, s], rt3[:, s], rt4[:, s],
                                            ALU.not_equal), "V", 1)
                W("vector", t6_)
                t7_ = bd.do("vector", lambda E, s=sl8:
                            E.tensor_tensor(rt4[:, s], rt4[:, s], rt5[:, s],
                                            ALU.add), "V", 1)
                W("vector", t7_)
                t8_ = bd.do("vector", lambda E, s=sl8:
                            E.tensor_tensor(rt4[:, s], rt0[:, s], rt4[:, s],
                                            ALU.mult), "V", 1)
                W("vector", t8_)
                t9_ = bd.do("vector", lambda E, s=sl8, ts=ts:
                            E.reduce_sum(rt1[:, 48 + ts:49 + ts], rt4[:, s], AX.X),
                            "V", 1)
                W("vector", t9_)
                ta2 = bd.do("vector", lambda E, ts=ts:
                            E.tensor_scalar_add(rt1[:, 48 + ts:49 + ts],
                                                rt1[:, 48 + ts:49 + ts], 1e-20), "V", 1)
                W("vector", ta2)
                tb2 = bd.do("vector", lambda E, ts=ts:
                            E.reciprocal(rt1[:, 48 + ts:49 + ts],
                                         rt1[:, 48 + ts:49 + ts]), "V", 1)
                W("vector", tb2)
                tc2 = bd.do("vector", lambda E, ts=ts:
                            E.tensor_scalar_mul(rt1[:, 48 + ts:49 + ts],
                                                rt1[:, 48 + ts:49 + ts], 2.5), "V", 1)
                W("vector", tc2)
                t_cw = bd.do("vector", lambda E, s=sl8, ts=ts:
                             E.tensor_scalar_mul(rt2[:, s], rt4[:, s],
                                                 rt1[:, 48 + ts:49 + ts]), "V", 1)
            lnsf_next = ffn_prep_ln(f + 1) if f + 1 < NF else None

            t_rout = [None] * 8
            for ih in range(2):
                t_sh_ev = None
                for ic8 in range(8):
                    ic = 8 * ih + ic8
                    sl = ic8 % 4
                    dws = load_w(wb, sl, "W%d" % sl, wsup[ic], ssup[ic], False,
                                 t_wb_prev[sl], sw_ffn_l[0])
                    W("tensor", dws, last["pmm%d" % (sl % 2)])
                    mms = None
                    for kc in range(KC):
                        mms = mm_triple(pmm[sl % 2][:, 0:TF],
                                        wsl(wb, sl, kc), wsl(wb, sl, kc, True),
                                        xnff_hi[:, TF * kc:TF * kc + TF],
                                        xnff_lo[:, TF * kc:TF * kc + TF],
                                        kc == 0, kc == KC - 1)
                    t_wb_prev[sl] = mms
                    s32 = scr32[:, TF * (ic8 % 2):TF * (ic8 % 2) + TF]
                    W("vector", mms, last["scr"] if ic8 == 0 else None,
                      t_s32[ic8 % 2])
                    tq1 = bd.do("vector", lambda E, sl=sl, s32=s32:
                                E.tensor_scalar_max(s32, pmm[sl % 2][:, 0:TF], 0.0),
                                "V", 1)
                    last["pmm%d" % (sl % 2)] = tq1
                    W("scalar", tq1)
                    tq2 = bd.do("scalar", lambda E, s32=s32:
                                E.square(s32, s32), "S", 1)
                    W("scalar", tq2)
                    tq3 = bd.do("scalar", lambda E, ic8=ic8, s32=s32:
                                E.copy(scr_hi[:, TF * ic8:TF * ic8 + TF], s32), "S", 1)
                    W("vector", tq3)
                    t_sh_ev = bd.do("vector", lambda E, ic8=ic8, s32=s32:
                                    E.tensor_sub(scr_lo[:, TF * ic8:TF * ic8 + TF],
                                                 s32,
                                                 scr_hi[:, TF * ic8:TF * ic8 + TF]),
                                    "V", 1)
                    t_s32[ic8 % 2] = t_sh_ev
                for oc in range(8):
                    sl = oc % 2
                    dwd = load_w(wbig, sl, "B%d" % sl, wsdown[oc, ih],
                                 ssdn[oc, ih], False, t_wbig_prev[sl], sw_ffn_l[0])
                    W("tensor", dwd, t_sh_ev, last["pmm%d" % (sl % 2)])
                    mmd = None
                    for ic8 in range(8):
                        mmd = mm_triple(pmm[sl % 2][:, 0:TF],
                                        wsl(wbig, sl, ic8), wsl(wbig, sl, ic8, True),
                                        scr_hi[:, TF * ic8:TF * ic8 + TF],
                                        scr_lo[:, TF * ic8:TF * ic8 + TF],
                                        ic8 == 0, ic8 == 7)
                    t_wbig_prev[sl] = mmd
                    if ih == 0:
                        W("vector", mmd, last["routed"] if oc == 0 else None)
                        trt = bd.do("vector", lambda E, oc=oc, sl=sl, f=f:
                                    E.tensor_tensor(routed[:, TF * oc:TF * oc + TF],
                                                    pmm[sl % 2][:, 0:TF],
                                                    acc[:, 2048 * oc + TF * f:
                                                        2048 * oc + TF * f + TF],
                                                    ALU.add), "V", 1)
                    else:
                        W("vector", mmd, t_rout[oc])
                        trt = bd.do("vector", lambda E, oc=oc, sl=sl:
                                    E.tensor_tensor(routed[:, TF * oc:TF * oc + TF],
                                                    pmm[sl % 2][:, 0:TF],
                                                    routed[:, TF * oc:TF * oc + TF],
                                                    ALU.add), "V", 1)
                    t_rout[oc] = trt
                    last["pmm%d" % (sl % 2)] = trt
                last["scr"] = bd.tok("T")

            t_cwfm = None
            for ts in range(4):
                W("tensor", t_cw, pv4_last[ts], last["cwfm"] if ts == 0 else None)
                tct = bd.do("tensor", lambda E, ts=ts:
                            E.transpose(pv4[ts][0:8, 0:128], rt2[:, 8 * ts:8 * ts + 8],
                                        ident[:]), "T", 1)
                W("vector", tct)
                t_cwfm = bd.do("vector", lambda E, ts=ts:
                               E.tensor_copy(cwfm[0:8, 128 * ts:128 * ts + 128],
                                             pv4[ts][0:8, 0:128]), "V", 1)
                pv4_last[ts] = t_cwfm

            for ep in range(4):
                t_r2 = [None] * 8
                for e2 in range(2):
                    e = 2 * ep + e2
                    W("tensor", t_cwfm, last["p2"], t_selm)
                    mcw = bd.do("tensor", lambda E, e=e:
                                E.matmul(p2[:, 0:TF], selm_sb[:, 128 * e:128 * e + 128],
                                         cwfm[0:8, :], start=True, stop=True), "T", 1)
                    t_p2u = None
                    for ic in range(4):
                        sl = ic % 4
                        dwu = load_w(wb, sl, "W%d" % sl, wup[e, ic], sup_s[e, ic],
                                     False, t_wb_prev[sl], sw_ffn_l[0])
                        W("tensor", dwu, last["pmm%d" % (sl % 2)])
                        mmu = None
                        for kc in range(KC):
                            mmu = mm_triple(pmm[sl % 2][:, 0:TF],
                                            wsl(wb, sl, kc), wsl(wb, sl, kc, True),
                                            xnff_hi[:, TF * kc:TF * kc + TF],
                                            xnff_lo[:, TF * kc:TF * kc + TF],
                                            kc == 0, kc == KC - 1)
                        t_wb_prev[sl] = mmu
                        ch = 4 * e2 + ic
                        s32 = scr32[:, TF * (ic % 2):TF * (ic % 2) + TF]
                        W("vector", mmu, t_s32[ic % 2],
                          last["scr"] if (e2 == 0 and ic == 0) else None)
                        tq1 = bd.do("vector", lambda E, sl=sl, s32=s32:
                                    E.tensor_scalar_max(s32, pmm[sl % 2][:, 0:TF], 0.0),
                                    "V", 1)
                        last["pmm%d" % (sl % 2)] = tq1
                        W("scalar", tq1)
                        tq2 = bd.do("scalar", lambda E, s32=s32:
                                    E.square(s32, s32), "S", 1)
                        W("vector", tq2, mcw)
                        tq3 = bd.do("vector", lambda E, s32=s32:
                                    E.tensor_tensor(s32, s32, p2[:, 0:TF], ALU.mult),
                                    "V", 1)
                        t_p2u = tq3
                        W("scalar", tq3)
                        tq4 = bd.do("scalar", lambda E, ch=ch, s32=s32:
                                    E.copy(scr_hi[:, TF * ch:TF * ch + TF], s32),
                                    "S", 1)
                        W("vector", tq4)
                        tre = bd.do("vector", lambda E, ch=ch, s32=s32:
                                    E.tensor_sub(scr_lo[:, TF * ch:TF * ch + TF],
                                                 s32,
                                                 scr_hi[:, TF * ch:TF * ch + TF]),
                                    "V", 1)
                        t_r2[ch] = tre
                        t_s32[ic % 2] = tre
                    last["p2"] = t_p2u
                for oc in range(8):
                    sl = oc % 2
                    dwd = load_w(wbig, sl, "B%d" % sl, wdown[oc, ep],
                                 sdn_s[oc, ep], False, t_wbig_prev[sl], sw_ffn_l[0])
                    W("tensor", dwd, last["pmm%d" % (sl % 2)])
                    mmd = None
                    for ch in range(8):
                        if oc == 0:
                            W("tensor", t_r2[ch])
                        mmd = mm_triple(pmm[sl % 2][:, 0:TF],
                                        wsl(wbig, sl, ch), wsl(wbig, sl, ch, True),
                                        scr_hi[:, TF * ch:TF * ch + TF],
                                        scr_lo[:, TF * ch:TF * ch + TF],
                                        ch == 0, ch == 7)
                    t_wbig_prev[sl] = mmd
                    W("vector", mmd, t_rout[oc])
                    trt = bd.do("vector", lambda E, oc=oc, sl=sl:
                                E.tensor_tensor(routed[:, TF * oc:TF * oc + TF],
                                                pmm[sl % 2][:, 0:TF],
                                                routed[:, TF * oc:TF * oc + TF],
                                                ALU.add), "V", 1)
                    t_rout[oc] = trt
                    last["pmm%d" % (sl % 2)] = trt
                last["scr"] = bd.tok("T")
            last["cwfm"] = bd.tok("T")

            if f + 1 < NF:
                last["xnff"] = bd.tok("T")
                t_xnff_next = ffn_prep_tr(lnsf_next)

            p3fin = [None] * 4
            for tsub in range(4):
                osl = tsub % 2
                W("sync", t_xres_prev[osl])
                dxs = bd.do("sync", lambda E, f=f, tsub=tsub, osl=osl:
                            E.dma_start(out=xres[:, 1024 * osl:1024 * osl + 1024],
                                        in_=x_res[512 * f + 128 * tsub:
                                                  512 * f + 128 * tsub + 128, :]),
                            "XR%d" % osl, 16)
                tfe = None
                pfin = [p3, p2]
                pfin_tok = [last["p3"], last["p2"]]
                for oc in range(8):
                    W("tensor", t_rout[oc] if tsub == 0 else None,
                      pfin_tok[oc % 2] if oc < 2 else None, p3fin[oc % 2])
                    tft = bd.do("tensor", lambda E, oc=oc, tsub=tsub:
                                E.transpose(pfin[oc % 2][:, 0:128],
                                            routed[:, TF * oc + 128 * tsub:
                                                   TF * oc + 128 * tsub + 128],
                                            ident[:]), "T", 1)
                    W("vector", tft, t_outp_prev[osl] if oc == 0 else None,
                      dxs if oc == 0 else None)
                    tfe = bd.do("vector", lambda E, oc=oc, osl=osl:
                                E.tensor_tensor(
                                    outp[:, 1024 * osl + 128 * oc:
                                         1024 * osl + 128 * oc + 128],
                                    pfin[oc % 2][:, 0:128],
                                    xres[:, 1024 * osl + 128 * oc:
                                         1024 * osl + 128 * oc + 128],
                                    ALU.add), "V", 1)
                    p3fin[oc % 2] = tfe
                last["p3"] = p3fin[0]
                last["p2"] = p3fin[1]
                t_xres_prev[osl] = tfe
                W("gpsimd", tfe)
                t_outp_prev[osl] = bd.do("gpsimd", lambda E, f=f, tsub=tsub, osl=osl:
                                         E.dma_start(
                                             out=out_d[512 * f + 128 * tsub:
                                                       512 * f + 128 * tsub + 128, :],
                                             in_=outp[:, 1024 * osl:1024 * osl + 1024]),
                                         "O%d" % (osl), 16)
            last["routed"] = bd.tok("T")
            if f + 1 < NF:
                t_xnff = t_xnff_next

        for k in ("I", "X", "W0", "W1", "W2", "W3", "WV0", "WV1", "B0", "B1",
                  "SHQ", "SHK", "PV",
                  "PV2", "O0", "O1", "F0", "F1", "SW0", "SW1", "XR0", "XR1",
                  "G0", "G1"):
            bd.W("gpsimd", (k, bd.cnt[k]))

        @block.sync
        def _(E):
            bd.replay("sync", E, sems)

        @block.gpsimd
        def _(E):
            bd.replay("gpsimd", E, sems)

        @block.tensor
        def _(E):
            bd.replay("tensor", E, sems)

        @block.vector
        def _(E):
            bd.replay("vector", E, sems)

        @block.scalar
        def _(E):
            bd.replay("scalar", E, sems)

    return nc


_NC = None


def _get_nc():
    global _NC
    if _NC is None:
        _NC = build_nc()
    return _NC


def _prep_shared(inputs):
    f32 = np.float32
    w_qkv = np.asarray(inputs["L0_attn_w_qkv"], f32)
    w_proj = np.asarray(inputs["L0_attn_w_proj"], f32)
    b_proj = np.asarray(inputs["L0_attn_b_proj"], f32)
    gate_w = np.asarray(inputs["L0_ffn_gate_weight"], f32)
    gate_b = np.asarray(inputs["L0_ffn_gate_bias"], f32)
    up = np.asarray(inputs["L0_ffn_up_weights"], f32)
    down = np.asarray(inputs["L0_ffn_down_weights"], f32)
    sup = np.asarray(inputs["L0_ffn_shared_up_weight"], f32)
    sdown = np.asarray(inputs["L0_ffn_shared_down_weight"], f32)

    C = np.ascontiguousarray
    sh = {}
    sh["wqk"] = C(w_qkv[:2048].reshape(16, 128, 8, 128).transpose(0, 3, 2, 1)
                  .reshape(16, 128, 1024))
    sh["wv"] = C(w_qkv[2048:].reshape(1024, 8, 128).transpose(1, 2, 0))
    sh["wproj"] = C(w_proj.reshape(8, 128, 8, 128).transpose(0, 3, 2, 1)
                    .reshape(8, 128, 1024))
    sh["wup"] = C(up.reshape(8, 4, 128, 8, 128).transpose(0, 1, 4, 3, 2)
                  .reshape(8, 4, 128, 1024))
    sh["wdown"] = C(down.reshape(4, 2, 8, 128, 4, 128).transpose(2, 0, 5, 1, 4, 3)
                    .reshape(8, 4, 128, 1024))
    sh["wsup"] = C(sup.reshape(16, 128, 8, 128).transpose(0, 3, 2, 1)
                   .reshape(16, 128, 1024))
    sh["wsdown"] = C(sdown.reshape(8, 128, 2, 8, 128).transpose(0, 2, 4, 3, 1)
                     .reshape(8, 2, 128, 1024))
    sh["wgate"] = C(gate_w.T.reshape(8, 128, 8).transpose(1, 0, 2).reshape(128, 64))
    sh["bproj"] = C(b_proj.reshape(8, 128).T)
    sh["gbias"] = C(np.broadcast_to(gate_b, (128, 8)))
    sh["identw"] = np.eye(128, dtype=f32)
    jpat = ((np.arange(16 * TA) % 128) // 16) % 8
    km = np.zeros((8, 16 * TA), f32)
    qm = np.zeros((8, 16 * TA), f32)
    for j0 in range(8):
        km[j0] = (jpat == j0).astype(f32)
        qm[j0] = -MASKC * (jpat != j0).astype(f32)
    sh["kmask"] = km
    sh["qmask"] = qm
    selmat = np.zeros((8, 1024), f32)
    for e in range(8):
        selmat[e, 128 * e:128 * e + 128] = 1.0
    sh["selm"] = selmat
    return sh


def kernel(**inputs):
    x = np.asarray(inputs["L0_attn_x"], np.float32)
    sh = _prep_shared(inputs)
    rr = 512 * np.arange(16)[:, None] + np.arange(128)[None, :]
    in_maps = []
    for c in range(8):
        b, jc = c // 4, c % 4
        m = dict(sh)
        m["x_att"] = np.ascontiguousarray(x[b, 2048 * jc:2048 * jc + 2048])
        m["x_res"] = np.ascontiguousarray(x[b][(rr + 128 * jc).ravel()])
        in_maps.append(m)
    nc = _get_nc()
    res = run_bass_kernel_spmd(nc, in_maps, core_ids=list(range(8)))
    out = np.empty((2, 8192, 1024), np.float32)
    for c in range(8):
        b, jc = c // 4, c % 4
        out[b][(rr + 128 * jc).ravel()] = res.results[c]["out"]
    return out


# version 14 (final)
# speedup vs baseline: 1.0162x; 1.0093x over previous
import numpy as np
import concourse.bass as bass
import concourse.mybir as mybir
from concourse.bass_utils import run_bass_kernel_spmd

F32 = mybir.dt.float32
BF16 = mybir.dt.bfloat16
ALU = mybir.AluOpType
AF = mybir.ActivationFunctionType
AX = mybir.AxisListType

D = 1024
H = 16
HD = 64
TC = 2048
NA = 8
TA = 256
NF = 4
TF = 512
KC = 8
MASKC = 200.0
EPS = 1e-5
ENGS = ("sync", "gpsimd", "tensor", "vector", "scalar")


def _raw(t):
    return t.tensor if hasattr(t, "tensor") else t


class Build:

    def __init__(self, semnames):
        self.cnt = {k: 0 for k in semnames}
        self.q = {e: [] for e in ENGS}

    def W(self, eng, *toks):
        for t in toks:
            if t is not None:
                self.q[eng].append(("w", t[0], t[1]))

    def do(self, eng, maker, sem=None, n=1):
        if sem is None or n == 0:
            self.q[eng].append(("i", maker, None, 0))
            return None
        self.cnt[sem] += n
        self.q[eng].append(("i", maker, sem, n))
        return (sem, self.cnt[sem])

    def tok(self, sem):
        return (sem, self.cnt[sem])

    def replay(self, eng, E, sems):
        for item in self.q[eng]:
            if item[0] == "w":
                E.wait_ge(sems[item[1]], item[2])
            else:
                _, maker, sem, n = item
                inst = maker(E)
                if sem is not None:
                    inst.then_inc(sems[sem], n)


def build_nc():
    nc = bass.Bass("TRN2", target_bir_lowering=False, debug=False)

    d_in = {}

    def din(name, shape):
        d_in[name] = nc.dram_tensor(name, list(shape), F32, kind="ExternalInput").ap()
        return d_in[name]

    x_att = din("x_att", [TC, D])
    x_res = din("x_res", [TC, D])
    wqk = din("wqk", [16, 128, 1024])
    wv = din("wv", [8, 128, 1024])
    wproj = din("wproj", [8, 128, 1024])
    wup = din("wup", [8, 4, 128, 1024])
    wdown = din("wdown", [8, 4, 128, 1024])
    wsup = din("wsup", [16, 128, 1024])
    wsdown = din("wsdown", [8, 2, 128, 1024])
    wgate = din("wgate", [128, 64])
    bproj = din("bproj", [128, 8])
    gbias = din("gbias", [128, 8])
    identw = din("identw", [128, 128])
    kmask = din("kmask", [8, 16 * TA])
    qmask = din("qmask", [8, 16 * TA])
    selm = din("selm", [8, 1024])
    out_d = nc.dram_tensor("out", [TC, D], F32, kind="ExternalOutput").ap()

    def scr_dram(name, shape):
        return nc.dram_tensor(name, list(shape), BF16, kind="Internal").ap()
    sqk = scr_dram("sqk", [16, 128, 2048])
    sv = scr_dram("sv", [8, 128, 2048])
    sproj = scr_dram("sproj", [8, 128, 2048])
    sup_s = scr_dram("sup_s", [8, 4, 128, 2048])
    sdn_s = scr_dram("sdn_s", [8, 4, 128, 2048])
    ssup = scr_dram("ssup", [16, 128, 2048])
    ssdn = scr_dram("ssdn", [8, 2, 128, 2048])

    from contextlib import ExitStack
    with ExitStack() as _st:
        acc = _st.enter_context(nc.sbuf_tensor("acc", [128, 8 * 2048], F32))
        ident = _st.enter_context(nc.sbuf_tensor("ident", [128, 128], F32))
        bproj_sb = _st.enter_context(nc.sbuf_tensor("bproj_sb", [128, 8], F32))
        gbias_sb = _st.enter_context(nc.sbuf_tensor("gbias_sb", [128, 8], F32))
        gate_sb = _st.enter_context(nc.sbuf_tensor("gate_sb", [128, 64], F32))
        wb = _st.enter_context(nc.sbuf_tensor("wb", [128, 4 * 2048], BF16))
        wbig = _st.enter_context(nc.sbuf_tensor("wbig", [128, 2 * 2048], BF16))
        gate_hi = _st.enter_context(nc.sbuf_tensor("gate_hi", [128, 64], BF16))
        gate_lo = _st.enter_context(nc.sbuf_tensor("gate_lo", [128, 64], BF16))
        stats = _st.enter_context(nc.sbuf_tensor("stats", [128, 48], F32))
        mv = _st.enter_context(nc.sbuf_tensor("mv", [128, 8], F32))
        rstd = _st.enter_context(nc.sbuf_tensor("rstd", [128, 4], F32))
        rzb = _st.enter_context(nc.sbuf_tensor("rzb", [128, 4], F32))
        ARF = 21280
        AR = _st.enter_context(nc.sbuf_tensor("AR", [128, ARF], F32))
        _off = [0]

        def take(parts, elems):
            o = _off[0]
            _off[0] += elems
            assert _off[0] <= ARF, "arena overflow"
            return AR[0:parts, o:o + elems], o
        ARB = 16384
        ARB_T = _st.enter_context(nc.sbuf_tensor("ARB", [128, ARB], BF16))
        _off16 = [0]

        def take16(parts, elems_bf16):
            o = _off16[0]
            _off16[0] += elems_bf16
            assert _off16[0] <= ARB, "bf16 arena overflow"
            return ARB_T[0:parts, o:o + elems_bf16], o
        xa, _ = take(128, 2 * 1024)
        oddst, _ = take(128, 8 * TA)
        wsf, _ = take(128, 2 * 1024)
        xnf_hi, _ = take16(128, 8 * TA)
        xnf_lo, _ = take16(128, 8 * TA)
        wvb, _ = take16(128, 2 * 2048)
        q_sr, _ = take(72, 16 * TA)
        k_sr, _ = take(72, 16 * TA)
        v_tok, c_vtok = take(128, 2 * 1024)
        v_pv, c_vpv = take(128, 2 * 16 * 65)
        Pt, _ = take(128, 512)
        o_sb, _ = take(128, 1024)
        o_mq_hi, _ = take16(128, 8 * TA)
        o_mq_lo, _ = take16(128, 8 * TA)
        spl, _ = take16(128, 2 * 2048)
        _off[0] = 0
        _off16[0] = 0
        xr, _ = take(128, 4 * 1024)
        wsf_f, _ = take(128, 2 * 1024)
        xnr, _ = take(128, 4 * 1024)
        xnff_hi, _ = take16(128, 8 * TF)
        xnff_lo, _ = take16(128, 8 * TF)
        lgt, _ = take(128, 32)
        rt0, _ = take(128, 32)
        rt1, _ = take(128, 64)
        rt2, _ = take(128, 32)
        rt3, _ = take(128, 32)
        rt4, _ = take(128, 32)
        rt5, _ = take(128, 32)
        cwfm, _ = take(128, TF)
        selm_sb, _ = take(8, 1024)
        scr_hi, _ = take16(128, 8 * TF)
        scr_lo, _ = take16(128, 8 * TF)
        scr32, _ = take(128, 2 * TF)
        routed, _ = take(128, 8 * TF)
        xres, _ = take(128, 2 * 1024)
        outp, _ = take(128, 2 * 1024)
        p0 = _st.enter_context(nc.psum_tensor("p0", [128, 512], F32))
        p1 = _st.enter_context(nc.psum_tensor("p1", [128, 512], F32))
        p2 = _st.enter_context(nc.psum_tensor("p2", [128, 512], F32))
        p3 = _st.enter_context(nc.psum_tensor("p3", [128, 512], F32))
        p4 = _st.enter_context(nc.psum_tensor("p4", [128, 512], F32))
        p5 = _st.enter_context(nc.psum_tensor("p5", [128, 512], F32))
        p6 = _st.enter_context(nc.psum_tensor("p6", [128, 512], F32))
        p7 = _st.enter_context(nc.psum_tensor("p7", [128, 512], F32))
        semnames = ["I", "X", "W0", "W1", "W2", "W3", "WV0", "WV1", "B0", "B1",
                    "SHQ", "SHK", "PV", "PV2", "O0", "O1",
                    "F0", "F1", "SW0", "SW1",
                    "XR0", "XR1", "G0", "G1", "T", "V", "S", "P"]
        sems = {k: _st.enter_context(nc.semaphore("s" + k)) for k in semnames}
        block = _st.enter_context(nc.Block())
        bd = Build(list(sems))
        W = bd.W
        pmm = [p0, p1]
        pv4 = [p4, p5, p6, p7]
        pv4_last = [None] * 4

        for ap_src, dst in ((identw, ident), (bproj, bproj_sb), (gbias, gbias_sb),
                            (wgate, gate_sb)):
            bd.do("sync", lambda E, s=ap_src, d=dst: E.dma_start(out=d[:], in_=s), "I", 16)
        bd.do("sync", lambda E: E.dma_start(out=k_sr[64:72, :], in_=kmask), "I", 16)
        bd.do("sync", lambda E: E.dma_start(out=q_sr[64:72, :], in_=qmask), "I", 16)
        t_init = bd.tok("I")
        W("tensor", t_init)
        W("vector", t_init)
        W("scalar", t_init)
        v_ones = bd.do("vector", lambda E: E.memset(v_pv[:], 1.0), "V", 1)
        tg1 = bd.do("scalar", lambda E: E.copy(gate_hi[:], gate_sb[:]), "S", 1)
        W("vector", tg1)
        t_gate = bd.do("vector", lambda E:
                       E.tensor_sub(gate_lo[:], gate_sb[:], gate_hi[:]), "V", 1)

        last = {
            "xa": None, "xnf": None, "q_sr": None, "k_sr": None, "oddst": None,
            "v_tok": None, "v_pv": None, "Pt": None, "o_sb": None, "o_mq": None,
            "pmm0": None, "pmm1": None, "p2": None, "p3": None,
            "xr": None, "xnr0": None, "xnff": None,
            "scr": None, "routed": None, "cwfm": None,
        }
        t_wb_prev = [None] * 4
        t_wv_prev = [None, None]
        t_wbig_prev = [None, None]
        t_outp_prev = [None, None]
        t_wsf_prev = [None] * 4
        t_wb_scrw = [None, None]
        t_spl_scrw = [None, None]

        def load_w(buf, sl, wsem, fp32_src, scratch_blk, first, reuse_tok,
                   sw_toks=None):
            if first:
                fs = sl % 2
                W("sync", t_wsf_prev[fs])
                dfi = bd.do("sync", lambda E, fs=fs, src=fp32_src:
                            E.dma_start(out=wsf[:, 1024 * fs:1024 * fs + 1024],
                                        in_=src), "F%d" % fs, 16)
                W("scalar", dfi, t_wb_scrw[fs], reuse_tok)
                thi = bd.do("scalar", lambda E, fs=fs, sl=sl, b=buf:
                            E.copy(b[:, 2048 * sl:2048 * sl + 1024],
                                   wsf[:, 1024 * fs:1024 * fs + 1024]), "S", 1)
                W("vector", thi)
                tlo = bd.do("vector", lambda E, fs=fs, sl=sl, b=buf:
                            E.tensor_sub(b[:, 2048 * sl + 1024:2048 * sl + 2048],
                                         wsf[:, 1024 * fs:1024 * fs + 1024],
                                         b[:, 2048 * sl:2048 * sl + 1024]), "V", 1)
                t_wsf_prev[fs] = tlo
                W("gpsimd", tlo)
                tsw = bd.do("gpsimd", lambda E, sl=sl, b=buf, dst=scratch_blk:
                            E.dma_start(out=dst, in_=b[:, 2048 * sl:2048 * sl + 2048]),
                            "SW%d" % (sl % 2), 16)
                t_wb_scrw[fs] = tsw
                return tlo
            else:
                W("sync", reuse_tok, *(sw_toks or ()))
                dw = bd.do("sync", lambda E, sl=sl, b=buf, src=scratch_blk:
                           E.dma_start(out=b[:, 2048 * sl:2048 * sl + 2048],
                                       in_=src), wsem, 16)
                return dw

        def presplit(fp32_src, scratch_blk, pi):
            fs = pi % 2
            ps = pi % 2
            W("sync", t_wsf_prev[fs])
            dfi = bd.do("sync", lambda E, fs=fs, src=fp32_src:
                        E.dma_start(out=wsf[:, 1024 * fs:1024 * fs + 1024],
                                    in_=src), "F%d" % fs, 16)
            W("scalar", dfi, t_spl_scrw[ps])
            thi = bd.do("scalar", lambda E, fs=fs, ps=ps:
                        E.copy(spl[:, 2048 * ps:2048 * ps + 1024],
                               wsf[:, 1024 * fs:1024 * fs + 1024]), "S", 1)
            W("vector", thi)
            tlo = bd.do("vector", lambda E, fs=fs, ps=ps:
                        E.tensor_sub(spl[:, 2048 * ps + 1024:2048 * ps + 2048],
                                     wsf[:, 1024 * fs:1024 * fs + 1024],
                                     spl[:, 2048 * ps:2048 * ps + 1024]), "P", 1)
            t_wsf_prev[fs] = tlo
            W("gpsimd", tlo)
            t_spl_scrw[ps] = bd.do("gpsimd", lambda E, ps=ps, dst=scratch_blk:
                                   E.dma_start(out=dst,
                                               in_=spl[:, 2048 * ps:2048 * ps + 2048]),
                                   "SW%d" % ps, 16)

        def wsl(buf, sl, kc, lo=False):
            o = 2048 * sl + (1024 if lo else 0) + 128 * kc
            return buf[:, o:o + 128]

        def mm_triple(psum_ap, a_hi, a_lo, b_hi, b_lo, first_flag, stop_flag,
                      inc=None):
            if inc is None:
                inc = stop_flag
            ops = [(a_hi, b_hi), (a_hi, b_lo), (a_lo, b_hi)]
            mmi = None
            for i, (wa, xa_) in enumerate(ops):
                mmi = bd.do("tensor", lambda E, wa=wa, xa_=xa_, i=i:
                            E.matmul(psum_ap, wa, xa_,
                                     start=(first_flag and i == 0),
                                     stop=(stop_flag and i == 2)),
                            "T", 1 if (inc and i == 2) else 0)
            return mmi

        def ln_subtile(src_sb, src_off, dst_sb, dst_off, st_i, v_deps):
            W("vector", *v_deps)
            bd.do("vector", lambda E, o=src_off, i=st_i:
                  E.bn_stats(stats[:, 12 * i:12 * i + 6], src_sb[:, o:o + 512]), "V", 1)
            t2 = bd.do("vector", lambda E, o=src_off, i=st_i:
                       E.bn_stats(stats[:, 12 * i + 6:12 * i + 12],
                                  src_sb[:, o + 512:o + 1024]), "V", 1)
            W("vector", t2)
            t3 = bd.do("vector", lambda E, i=st_i:
                       E.bn_aggr(mv[:, 2 * i:2 * i + 2], stats[:, 12 * i:12 * i + 12]),
                       "V", 1)
            W("vector", t3)
            t3b = bd.do("vector", lambda E, i=st_i:
                        E.tensor_scalar_add(rstd[:, i:i + 1],
                                            mv[:, 2 * i + 1:2 * i + 2], EPS), "V", 1)
            W("scalar", t3b)
            t4 = bd.do("scalar", lambda E, i=st_i:
                       E.activation(rstd[:, i:i + 1], rstd[:, i:i + 1], AF.Ln), "S", 1)
            W("vector", t4)
            t4b = bd.do("vector", lambda E, i=st_i:
                        E.tensor_scalar_mul(rstd[:, i:i + 1], rstd[:, i:i + 1], -0.5),
                        "V", 1)
            W("scalar", t4b)
            t5 = bd.do("scalar", lambda E, i=st_i:
                       E.activation(rstd[:, i:i + 1], rstd[:, i:i + 1], AF.Exp), "S", 1)
            W("vector", t5)
            t6 = bd.do("vector", lambda E, so=src_off, do_=dst_off, i=st_i,
                       s=src_sb, d=dst_sb:
                       E.tensor_scalar(d[:, do_:do_ + 1024], s[:, so:so + 1024],
                                       mv[:, 2 * i:2 * i + 1], rstd[:, i:i + 1],
                                       ALU.subtract, ALU.mult), "V", 1)
            return t6

        def transpose_subtile(src_sb, src_off, dst_hi, dst_lo, FT, col_off,
                              v_dep, reuse_tok):
            ev = None
            for c in range(8):
                W("tensor", v_dep, pv4_last[c % 4])
                tt = bd.do("tensor", lambda E, c=c, o=src_off, s=src_sb:
                           E.transpose(pv4[c % 4][:, 0:128],
                                       s[:, o + 128 * c:o + 128 * c + 128],
                                       ident[:]), "T", 1)
                W("scalar", tt, reuse_tok if c == 0 else None)
                eh = bd.do("scalar", lambda E, c=c, co=col_off, d=dst_hi, FT=FT:
                           E.copy(d[:, c * FT + co:c * FT + co + 128],
                                  pv4[c % 4][:, 0:128]), "S", 1)
                W("vector", eh, reuse_tok if c == 0 else None)
                ev = bd.do("vector", lambda E, c=c, co=col_off, dh=dst_hi, dl=dst_lo,
                           FT=FT:
                           E.tensor_sub(dl[:, c * FT + co:c * FT + co + 128],
                                        pv4[c % 4][:, 0:128],
                                        dh[:, c * FT + co:c * FT + co + 128]), "V", 1)
                pv4_last[c % 4] = ev
            return ev

        def att_prep_ln(a):
            W("sync", last["xa"])
            dx = None
            for ss in range(2):
                dx = bd.do("sync", lambda E, ss=ss, a=a:
                           E.dma_start(out=xa[:, 1024 * ss:1024 * ss + 1024],
                                       in_=x_att[256 * a + 128 * ss:
                                                 256 * a + 128 * ss + 128, :]),
                           "X", 16)
            lns = []
            for ss in range(2):
                lns.append(ln_subtile(xa, 1024 * ss, xa, 1024 * ss, ss,
                                      [dx, last["xa"]]))
            return lns

        def att_prep_tr(lns):
            t_xnf = None
            for ss in range(2):
                t_xnf = transpose_subtile(xa, 1024 * ss, xnf_hi, xnf_lo, TA,
                                          128 * ss, lns[ss],
                                          last["xnf"] if ss == 0 else None)
            last["xa"] = bd.tok("T")
            return t_xnf

        def ffn_prep_dma(f, guards_sync=()):
            W("sync", last["xr"], *guards_sync)
            dxr = None
            for tsub in range(4):
                dxr = bd.do("sync", lambda E, f=f, tsub=tsub:
                            E.dma_start(out=xr[:, 1024 * tsub:1024 * tsub + 1024],
                                        in_=x_res[512 * f + 128 * tsub:
                                                  512 * f + 128 * tsub + 128, :]),
                            "X", 16)
            return dxr

        def ffn_prep_lnonly(f, dxr, guards_v=()):
            lns = []
            for tsub in range(4):
                lns.append(ln_subtile(xr, 1024 * tsub, xnr, 1024 * tsub, tsub,
                                      [dxr, last["xnr0"]] + list(guards_v)))
            last["xr"] = lns[3]
            return lns

        def ffn_prep_ln(f, guards_sync=(), guards_v=()):
            dxr = ffn_prep_dma(f, guards_sync)
            return ffn_prep_lnonly(f, dxr, guards_v)

        def ffn_prep_tr(lns):
            t_x = None
            for tsub in range(4):
                t_x = transpose_subtile(xnr, 1024 * tsub, xnff_hi, xnff_lo, TF,
                                        128 * tsub, lns[tsub],
                                        last["xnff"] if tsub == 0 else None)
            last["xnr0"] = bd.tok("T")
            return t_x

        presplit_jobs = []
        for ic in range(16):
            presplit_jobs.append((wsup[ic], ssup[ic]))
        for e in range(8):
            for ic in range(4):
                presplit_jobs.append((wup[e, ic], sup_s[e, ic]))
        for oc in range(8):
            for ep in range(4):
                presplit_jobs.append((wdown[oc, ep], sdn_s[oc, ep]))
        for oc in range(8):
            for ih in range(2):
                presplit_jobs.append((wsdown[oc, ih], ssdn[oc, ih]))
        assert len(presplit_jobs) == 96

        lns0 = att_prep_ln(0)
        t_xnf = att_prep_tr(lns0)
        sw_att = [None]
        lnsf0 = [None]
        dxr_f0 = [None]
        t_xnff_f0 = [None]
        for a in range(NA):
            t_vmm = None
            for kc in range(KC):
                sl = kc % 2
                dwv = load_w(wvb, sl, "WV%d" % sl, wv[kc], sv[kc], a == 0,
                             t_wv_prev[sl], sw_att[0])
                W("tensor", dwv)
                if kc == 0:
                    W("tensor", *pv4_last)
                    W("tensor", last["v_tok"])
                for tsub in range(2):
                    for vc in range(2):
                        t_vmm = mm_triple(
                            pv4[tsub * 2 + vc][:, 0:512],
                            xnf_hi[:, TA * kc + 128 * tsub:TA * kc + 128 * tsub + 128],
                            xnf_lo[:, TA * kc + 128 * tsub:TA * kc + 128 * tsub + 128],
                            wvb[:, 2048 * sl + 512 * vc:2048 * sl + 512 * vc + 512],
                            wvb[:, 2048 * sl + 1024 + 512 * vc:
                                2048 * sl + 1024 + 512 * vc + 512],
                            kc == 0, kc == KC - 1,
                            inc=(tsub == 1 and vc == 1))
                t_wv_prev[sl] = t_vmm
            W("scalar", t_vmm, last["v_tok"])
            for tsub in range(2):
                for vc in range(2):
                    tv = bd.do("scalar", lambda E, tsub=tsub, vc=vc:
                               E.copy(v_tok[:, 1024 * tsub + 512 * vc:
                                             1024 * tsub + 512 * vc + 512],
                                      pv4[tsub * 2 + vc][:, 0:512]), "S", 1)
                    pv4_last[tsub * 2 + vc] = tv
            t_vev = tv
            lns_next = att_prep_ln(a + 1) if a + 1 < NA else None
            t_vpv = {"PV": None, "PV2": None}

            def vpv_issue(gg):
                g, tsub = gg // 2, gg % 2
                sem = "PV" if gg % 2 == 0 else "PV2"
                if gg == 0:
                    W("scalar", t_vev, v_ones, last["v_pv"])
                t_vpv[sem] = bd.do("scalar", lambda E, g=g, tsub=tsub:
                                   E.dma_start(
                                       out=bass.AP(_raw(AR),
                                                   8 * g * ARF + c_vpv + 1040 * tsub,
                                                   [[ARF, 8], [65, 16], [1, 64]]),
                                       in_=bass.AP(_raw(AR),
                                                   g * 64 + c_vtok + 1024 * tsub,
                                                   [[ARF, 128], [1, 64]])),
                                   sem, 16)

            t_shift = {"q": None, "k": None}
            t_evac_qk = None
            for oc in range(16):
                sl = oc % 4
                dw = load_w(wb, sl, "W%d" % sl, wqk[oc], sqk[oc], a == 0,
                            t_wb_prev[sl], sw_att[0])
                W("tensor", dw, t_xnf, last["pmm%d" % (sl % 2)])
                mm = None
                for kc in range(KC):
                    mm = mm_triple(pmm[sl % 2][:, 0:TA],
                                   wsl(wb, sl, kc), wsl(wb, sl, kc, True),
                                   xnf_hi[:, TA * kc:TA * kc + TA],
                                   xnf_lo[:, TA * kc:TA * kc + TA],
                                   kc == 0, kc == KC - 1)
                t_wb_prev[sl] = mm
                dst = q_sr if oc < 8 else k_sr
                c = oc % 8
                W("scalar", mm,
                  last["q_sr"] if oc == 0 else None,
                  t_shift["q"] if oc == 8 else None,
                  last["oddst"] if oc == 0 else None)
                if oc < 8:
                    W("vector", mm,
                      last["q_sr"] if oc == 0 else None,
                      last["oddst"] if oc == 0 else None)
                    e1 = bd.do("vector", lambda E, c=c, sl=sl, dst=dst:
                               E.tensor_scalar_mul(
                                   dst.rearrange("p (t2 h t) -> p t2 h t", t2=2, h=16)
                                      [0:64, :, 2 * c, :],
                                   pmm[sl % 2][0:64, 0:TA].rearrange("p (t2 t) -> p t2 t", t2=2),
                                   0.125), "V", 1)
                    e2 = bd.do("vector", lambda E, c=c, sl=sl:
                               E.tensor_scalar_mul(
                                   oddst.rearrange("p (t2 c t) -> p t2 c t", t2=2, c=8)
                                        [64:128, :, c, :],
                                   pmm[sl % 2][64:128, 0:TA].rearrange("p (t2 t) -> p t2 t", t2=2),
                                   0.125), "V", 1)
                else:
                    e1 = bd.do("scalar", lambda E, c=c, sl=sl, dst=dst:
                               E.copy(dst.rearrange("p (t2 h t) -> p t2 h t", t2=2, h=16)
                                         [0:64, :, 2 * c, :],
                                      pmm[sl % 2][0:64, 0:TA].rearrange("p (t2 t) -> p t2 t", t2=2)),
                               "S", 1)
                    e2 = bd.do("scalar", lambda E, c=c, sl=sl:
                               E.copy(oddst.rearrange("p (t2 c t) -> p t2 c t", t2=2, c=8)
                                           [64:128, :, c, :],
                                      pmm[sl % 2][64:128, 0:TA].rearrange("p (t2 t) -> p t2 t", t2=2)),
                               "S", 1)
                last["pmm%d" % (sl % 2)] = e2
                t_evac_qk = e2
                vpv_issue(2 * oc)
                vpv_issue(2 * oc + 1)
                if a >= 2:
                    pj = 16 * (a - 2) + oc
                    presplit(presplit_jobs[pj][0], presplit_jobs[pj][1], pj)
                if oc in (7, 15):
                    key = "q" if oc == 7 else "k"
                    eng, shsem = ("sync", "SHQ") if oc == 7 else ("scalar", "SHK")
                    W(eng, e2, last["q_sr"])
                    t_shift[key] = bd.do(eng, lambda E, dst=dst:
                                         E.dma_start(
                                             out=dst.rearrange("p (t2 h t) -> p t2 h t",
                                                               t2=2, h=16)[0:64, :, 1::2, :],
                                             in_=oddst[64:128, :]), shsem, 16)
            last["oddst"] = t_shift["k"]
            t_vpv = (t_vpv["PV"], t_vpv["PV2"])
            last["v_tok"] = t_vpv[0]

            if a + 1 < NA:
                last["xnf"] = bd.tok("T")
                t_xnf_next = att_prep_tr(lns_next)
            if a == NA - 1:
                tqk7 = bd.tok("T")
                dxr_f0[0] = ffn_prep_dma(0, guards_sync=(tqk7, t_shift["q"],
                                                         t_shift["k"]))

            for tt in range(2):
                for b4 in range(4):
                    W("tensor", t_shift["q"], t_shift["k"], t_evac_qk,
                      last["p3"])
                    mmS = None
                    for slq in range(4):
                        vi = 4 * b4 + slq
                        mmS = bd.do("tensor", lambda E, vi=vi, slq=slq, tt=tt:
                                    E.matmul(p3[:, 128 * slq:128 * slq + 128],
                                             k_sr.rearrange("p (t2 g t) -> p t2 g t",
                                                            t2=2, g=16)
                                                 [:, tt, :, vi:vi + 113:16],
                                             q_sr.rearrange("p (t2 h t) -> p t2 h t",
                                                            t2=2, h=16)
                                                 [:, tt, :, vi:vi + 113:16],
                                             start=True, stop=True),
                                    "T", 1 if slq == 3 else 0)
                    W("scalar", mmS, last["Pt"])
                    t_exp = bd.do("scalar", lambda E:
                                  E.activation(Pt[:], p3[:], AF.Exp), "S", 1)
                    last["p3"] = t_exp
                    W("tensor", t_exp, last["p2"], *t_vpv)
                    mmPV = None
                    for slq in range(4):
                        vi = 4 * b4 + slq
                        mmPV = bd.do("tensor", lambda E, vi=vi, slq=slq, tt=tt:
                                     E.matmul(p2[:, 65 * slq:65 * slq + 65],
                                              Pt[:, 128 * slq:128 * slq + 128],
                                              v_pv[:, 1040 * tt + 65 * vi:
                                                   1040 * tt + 65 * vi + 65],
                                              start=True, stop=True),
                                     "T", 1 if slq == 3 else 0)
                    last["Pt"] = mmPV
                    W("vector", mmPV, last["o_sb"] if b4 == 0 else None)
                    tr1 = bd.do("vector", lambda E:
                                E.reciprocal(rzb[:], bass.AP(_raw(p2), 64,
                                                             [[512, 128], [65, 4]])),
                                "V", 1)
                    W("vector", tr1)
                    t_norm = bd.do("vector", lambda E, b4=b4:
                                   E.tensor_tensor(
                                       o_sb.rearrange("p (v hd) -> p v hd", hd=64)
                                           [:, 4 * b4:4 * b4 + 4, :],
                                       bass.AP(_raw(p2), 0, [[512, 128], [65, 4], [1, 64]]),
                                       rzb[:].unsqueeze(2).broadcast_to([128, 4, 64]),
                                       ALU.mult), "V", 1)
                    last["p2"] = t_norm
                for b in range(8):
                    W("tensor", t_norm, pv4_last[b % 4])
                    ttr = bd.do("tensor", lambda E, b=b:
                                E.transpose(pv4[b % 4][:, 0:128],
                                            o_sb[:, 128 * b:128 * b + 128], ident[:]),
                                "T", 1)
                    W("scalar", ttr, last["o_mq"] if (tt == 0 and b == 0) else None)
                    teh = bd.do("scalar", lambda E, b=b, tt=tt:
                                E.copy(
                                    o_mq_hi.rearrange("p (m hx) -> p m hx", m=8)[:, b, :]
                                           .rearrange("p (h j) -> p h j", h=16)
                                           [:, :, 8 * tt:8 * tt + 8],
                                    pv4[b % 4][:, 0:128]
                                        .rearrange("p (h j) -> p h j", h=16)), "S", 1)
                    W("vector", teh, last["o_mq"] if (tt == 0 and b == 0) else None)
                    tev = bd.do("vector", lambda E, b=b, tt=tt:
                                E.tensor_sub(
                                    o_mq_lo.rearrange("p (m hx) -> p m hx", m=8)[:, b, :]
                                           .rearrange("p (h j) -> p h j", h=16)
                                           [:, :, 8 * tt:8 * tt + 8],
                                    pv4[b % 4][:, 0:128]
                                        .rearrange("p (h j) -> p h j", h=16),
                                    o_mq_hi.rearrange("p (m hx) -> p m hx", m=8)[:, b, :]
                                           .rearrange("p (h j) -> p h j", h=16)
                                           [:, :, 8 * tt:8 * tt + 8]), "V", 1)
                    pv4_last[b % 4] = tev
                last["o_sb"] = bd.tok("T")
            last["v_pv"] = bd.tok("T")

            if a == NA - 1:
                tatt7 = bd.tok("T")
                lnsf0[0] = ffn_prep_lnonly(0, dxr_f0[0], guards_v=(tatt7,))
                t_xnff_f0[0] = ffn_prep_tr(lnsf0[0])

            t_omq_ev = bd.tok("V")
            for oc in range(8):
                sl = oc % 2
                dwp = load_w(wbig, sl, "B%d" % sl, wproj[oc], sproj[oc], a == 0,
                             t_wbig_prev[sl], sw_att[0])
                W("tensor", dwp, t_omq_ev if oc == 0 else None, last["pmm%d" % (sl % 2)])
                mmp = None
                for m in range(8):
                    mmp = mm_triple(pmm[sl % 2][:, 0:TA],
                                    wsl(wbig, sl, m), wsl(wbig, sl, m, True),
                                    o_mq_hi[:, TA * m:TA * m + TA],
                                    o_mq_lo[:, TA * m:TA * m + TA],
                                    m == 0, m == 7)
                t_wbig_prev[sl] = mmp
                W("vector", mmp)
                tpe = bd.do("vector", lambda E, oc=oc, sl=sl, a=a:
                            E.tensor_scalar(
                                acc.rearrange("p (o r) -> p o r", o=8)[:, oc, :]
                                   .rearrange("p (h g) -> p h g", h=16)
                                   [:, :, 16 * a:16 * a + 16],
                                pmm[sl % 2][:, 0:TA].rearrange("p (h g) -> p h g", h=16),
                                bproj_sb[:, oc:oc + 1], None, ALU.add), "V", 1)
                last["pmm%d" % (sl % 2)] = tpe
            last["o_mq"] = bd.tok("T")
            last["q_sr"] = bd.tok("T")
            if a == 0:
                sw_att[0] = (bd.tok("SW0"), bd.tok("SW1"))
            if a + 1 < NA:
                t_xnf = t_xnf_next

        t_acc_done = bd.tok("V")
        sw_ffn_l = [(bd.tok("SW0"), bd.tok("SW1"))]

        t_s32 = [None, None]
        t_xres_prev = [None, None]
        bd.do("sync", lambda E: E.dma_start(out=selm_sb, in_=selm), "I", 16)
        t_selm = bd.tok("I")

        t_xnff = t_xnff_f0[0]
        for f in range(NF):
            W("tensor", t_xnff, last["p2"], t_gate)
            mlg = None
            for tsub in range(4):
                for kc in range(KC):
                    mlg = mm_triple(p2[:, 8 * tsub:8 * tsub + 8],
                                    xnff_hi[:, TF * kc + 128 * tsub:
                                            TF * kc + 128 * tsub + 128],
                                    xnff_lo[:, TF * kc + 128 * tsub:
                                            TF * kc + 128 * tsub + 128],
                                    gate_hi[:, 8 * kc:8 * kc + 8],
                                    gate_lo[:, 8 * kc:8 * kc + 8],
                                    kc == 0, kc == KC - 1)
            mlg = mlg
            W("vector", mlg)
            tlg = bd.do("vector", lambda E: E.tensor_copy(lgt[:], p2[:, 0:32]), "V", 1)
            last["p2"] = tlg
            W("vector", tlg)
            ts0 = bd.do("vector", lambda E:
                        E.tensor_scalar_mul(rt0[:], lgt[:], -1.0), "V", 1)
            W("scalar", ts0)
            ts1 = bd.do("scalar", lambda E:
                        E.activation(rt0[:], rt0[:], AF.Exp), "S", 1)
            W("vector", ts1)
            bd.do("vector", lambda E: E.tensor_scalar_add(rt0[:], rt0[:], 1.0), "V", 1)
            W("vector", bd.tok("V"))
            tv2 = bd.do("vector", lambda E: E.reciprocal(rt0[:], rt0[:]), "V", 1)
            W("vector", tv2)
            tv3 = bd.do("vector", lambda E:
                        E.tensor_tensor(rt2.rearrange("p (t e) -> p t e", t=4),
                                        rt0.rearrange("p (t e) -> p t e", t=4),
                                        gbias_sb[:].unsqueeze(1).broadcast_to([128, 4, 8]),
                                        ALU.add), "V", 1)
            g3 = rt2.rearrange("p (t g e) -> p t g e", t=4, g=2)
            r8 = rt1.rearrange("p (k x) -> p k x", k=8)

            def rr(i):
                return r8[:, i, :].rearrange("p (t g) -> p t g", t=4)

            def tt_op(dst_off, in0, in1, op):
                return bd.do("vector", lambda E, d=dst_off, a0=in0, a1=in1, op=op:
                             E.tensor_tensor(rr(d) if True else None, a0, a1, op), "V", 1)
            W("vector", tv3)
            tt_op(0, g3[:, :, :, 0], g3[:, :, :, 1], ALU.max)
            tt_op(1, g3[:, :, :, 0], g3[:, :, :, 1], ALU.min)
            tt_op(2, g3[:, :, :, 2], g3[:, :, :, 3], ALU.max)
            td_ = tt_op(3, g3[:, :, :, 2], g3[:, :, :, 3], ALU.min)
            W("vector", td_)
            tt_op(4, rr(0), rr(2), ALU.max)
            tt_op(5, rr(0), rr(2), ALU.min)
            tg_ = tt_op(6, rr(1), rr(3), ALU.max)
            W("vector", tg_)
            th_ = tt_op(5, rr(5), rr(6), ALU.max)
            W("vector", th_)
            ti_ = tt_op(4, rr(4), rr(5), ALU.add)
            gs = rr(4)
            W("vector", ti_)
            tj_ = bd.do("vector", lambda E:
                        E.tensor_tensor(rt1[:, 48:52], gs[:, :, 0], gs[:, :, 1],
                                        ALU.is_ge), "V", 1)
            W("vector", tj_)
            tk_ = bd.do("vector", lambda E:
                        E.tensor_scalar(rt1[:, 52:56], rt1[:, 48:52], -1.0, 1.0,
                                        ALU.mult, ALU.add), "V", 1)
            W("vector", tk_)
            bd.do("vector", lambda E:
                  E.tensor_tensor(rt3.rearrange("p (t g e) -> p t g e", t=4, g=2)
                                  [:, :, 0, :],
                                  g3[:, :, 0, :],
                                  rt1[:, 48:52].unsqueeze(2).broadcast_to([128, 4, 4]),
                                  ALU.mult), "V", 1)
            tm1 = bd.do("vector", lambda E:
                        E.tensor_tensor(rt3.rearrange("p (t g e) -> p t g e", t=4, g=2)
                                        [:, :, 1, :],
                                        g3[:, :, 1, :],
                                        rt1[:, 52:56].unsqueeze(2).broadcast_to([128, 4, 4]),
                                        ALU.mult), "V", 1)
            W("vector", tm1)
            bd.do("vector", lambda E: E.memset(rt1[:, 57:64], -2e30), "V", 1)
            W("vector", bd.tok("V"))
            t_cw = None
            for ts in range(4):
                sl8 = slice(8 * ts, 8 * ts + 8)
                t1_ = bd.do("vector", lambda E, s=sl8:
                            E.reduce_max(rt1[:, 56:57], rt3[:, s], AX.X), "V", 1)
                W("vector", t1_)
                t2_ = bd.do("vector", lambda E, s=sl8:
                            E.match_replace(rt4[:, s], rt1[:, 56:64],
                                            rt3[:, s], -1e30), "V", 1)
                W("vector", t2_)
                t3_ = bd.do("vector", lambda E, s=sl8:
                            E.reduce_max(rt1[:, 56:57], rt4[:, s], AX.X), "V", 1)
                W("vector", t3_)
                t4_ = bd.do("vector", lambda E, s=sl8:
                            E.match_replace(rt5[:, s], rt1[:, 56:64],
                                            rt4[:, s], -1e30), "V", 1)
                W("vector", t4_)
                t5_ = bd.do("vector", lambda E, s=sl8:
                            E.tensor_tensor(rt5[:, s], rt4[:, s], rt5[:, s],
                                            ALU.not_equal), "V", 1)
                W("vector", t5_)
                t6_ = bd.do("vector", lambda E, s=sl8:
                            E.tensor_tensor(rt4[:, s], rt3[:, s], rt4[:, s],
                                            ALU.not_equal), "V", 1)
                W("vector", t6_)
                t7_ = bd.do("vector", lambda E, s=sl8:
                            E.tensor_tensor(rt4[:, s], rt4[:, s], rt5[:, s],
                                            ALU.add), "V", 1)
                W("vector", t7_)
                t8_ = bd.do("vector", lambda E, s=sl8:
                            E.tensor_tensor(rt4[:, s], rt0[:, s], rt4[:, s],
                                            ALU.mult), "V", 1)
                W("vector", t8_)
                t9_ = bd.do("vector", lambda E, s=sl8, ts=ts:
                            E.reduce_sum(rt1[:, 48 + ts:49 + ts], rt4[:, s], AX.X),
                            "V", 1)
                W("vector", t9_)
                ta2 = bd.do("vector", lambda E, ts=ts:
                            E.tensor_scalar_add(rt1[:, 48 + ts:49 + ts],
                                                rt1[:, 48 + ts:49 + ts], 1e-20), "V", 1)
                W("vector", ta2)
                tb2 = bd.do("vector", lambda E, ts=ts:
                            E.reciprocal(rt1[:, 48 + ts:49 + ts],
                                         rt1[:, 48 + ts:49 + ts]), "V", 1)
                W("vector", tb2)
                tc2 = bd.do("vector", lambda E, ts=ts:
                            E.tensor_scalar_mul(rt1[:, 48 + ts:49 + ts],
                                                rt1[:, 48 + ts:49 + ts], 2.5), "V", 1)
                W("vector", tc2)
                t_cw = bd.do("vector", lambda E, s=sl8, ts=ts:
                             E.tensor_scalar_mul(rt2[:, s], rt4[:, s],
                                                 rt1[:, 48 + ts:49 + ts]), "V", 1)
            lnsf_next = ffn_prep_ln(f + 1) if f + 1 < NF else None

            t_rout = [None] * 8
            for ih in range(2):
                t_sh_ev = None
                for ic8 in range(8):
                    ic = 8 * ih + ic8
                    sl = ic8 % 4
                    dws = load_w(wb, sl, "W%d" % sl, wsup[ic], ssup[ic], False,
                                 t_wb_prev[sl], sw_ffn_l[0])
                    W("tensor", dws, last["pmm%d" % (sl % 2)])
                    mms = None
                    for kc in range(KC):
                        mms = mm_triple(pmm[sl % 2][:, 0:TF],
                                        wsl(wb, sl, kc), wsl(wb, sl, kc, True),
                                        xnff_hi[:, TF * kc:TF * kc + TF],
                                        xnff_lo[:, TF * kc:TF * kc + TF],
                                        kc == 0, kc == KC - 1)
                    t_wb_prev[sl] = mms
                    s32 = scr32[:, TF * (ic8 % 2):TF * (ic8 % 2) + TF]
                    W("vector", mms, last["scr"] if ic8 == 0 else None,
                      t_s32[ic8 % 2])
                    tq1 = bd.do("vector", lambda E, sl=sl, s32=s32:
                                E.tensor_scalar_max(s32, pmm[sl % 2][:, 0:TF], 0.0),
                                "V", 1)
                    last["pmm%d" % (sl % 2)] = tq1
                    W("scalar", tq1)
                    tq2 = bd.do("scalar", lambda E, s32=s32:
                                E.square(s32, s32), "S", 1)
                    W("scalar", tq2)
                    tq3 = bd.do("scalar", lambda E, ic8=ic8, s32=s32:
                                E.copy(scr_hi[:, TF * ic8:TF * ic8 + TF], s32), "S", 1)
                    W("vector", tq3)
                    t_sh_ev = bd.do("vector", lambda E, ic8=ic8, s32=s32:
                                    E.tensor_sub(scr_lo[:, TF * ic8:TF * ic8 + TF],
                                                 s32,
                                                 scr_hi[:, TF * ic8:TF * ic8 + TF]),
                                    "V", 1)
                    t_s32[ic8 % 2] = t_sh_ev
                for oc in range(8):
                    sl = oc % 2
                    dwd = load_w(wbig, sl, "B%d" % sl, wsdown[oc, ih],
                                 ssdn[oc, ih], False, t_wbig_prev[sl], sw_ffn_l[0])
                    W("tensor", dwd, t_sh_ev, last["pmm%d" % (sl % 2)])
                    mmd = None
                    for ic8 in range(8):
                        mmd = mm_triple(pmm[sl % 2][:, 0:TF],
                                        wsl(wbig, sl, ic8), wsl(wbig, sl, ic8, True),
                                        scr_hi[:, TF * ic8:TF * ic8 + TF],
                                        scr_lo[:, TF * ic8:TF * ic8 + TF],
                                        ic8 == 0, ic8 == 7)
                    t_wbig_prev[sl] = mmd
                    if ih == 0:
                        W("vector", mmd, last["routed"] if oc == 0 else None)
                        trt = bd.do("vector", lambda E, oc=oc, sl=sl, f=f:
                                    E.tensor_tensor(routed[:, TF * oc:TF * oc + TF],
                                                    pmm[sl % 2][:, 0:TF],
                                                    acc[:, 2048 * oc + TF * f:
                                                        2048 * oc + TF * f + TF],
                                                    ALU.add), "V", 1)
                    else:
                        W("vector", mmd, t_rout[oc])
                        trt = bd.do("vector", lambda E, oc=oc, sl=sl:
                                    E.tensor_tensor(routed[:, TF * oc:TF * oc + TF],
                                                    pmm[sl % 2][:, 0:TF],
                                                    routed[:, TF * oc:TF * oc + TF],
                                                    ALU.add), "V", 1)
                    t_rout[oc] = trt
                    last["pmm%d" % (sl % 2)] = trt
                last["scr"] = bd.tok("T")

            t_cwfm = None
            for ts in range(4):
                W("tensor", t_cw, pv4_last[ts], last["cwfm"] if ts == 0 else None)
                tct = bd.do("tensor", lambda E, ts=ts:
                            E.transpose(pv4[ts][0:8, 0:128], rt2[:, 8 * ts:8 * ts + 8],
                                        ident[:]), "T", 1)
                W("vector", tct)
                t_cwfm = bd.do("vector", lambda E, ts=ts:
                               E.tensor_copy(cwfm[0:8, 128 * ts:128 * ts + 128],
                                             pv4[ts][0:8, 0:128]), "V", 1)
                pv4_last[ts] = t_cwfm

            for ep in range(4):
                t_r2 = [None] * 8
                for e2 in range(2):
                    e = 2 * ep + e2
                    W("tensor", t_cwfm, last["p2"], t_selm)
                    mcw = bd.do("tensor", lambda E, e=e:
                                E.matmul(p2[:, 0:TF], selm_sb[:, 128 * e:128 * e + 128],
                                         cwfm[0:8, :], start=True, stop=True), "T", 1)
                    t_p2u = None
                    for ic in range(4):
                        sl = ic % 4
                        dwu = load_w(wb, sl, "W%d" % sl, wup[e, ic], sup_s[e, ic],
                                     False, t_wb_prev[sl], sw_ffn_l[0])
                        W("tensor", dwu, last["pmm%d" % (sl % 2)])
                        mmu = None
                        for kc in range(KC):
                            mmu = mm_triple(pmm[sl % 2][:, 0:TF],
                                            wsl(wb, sl, kc), wsl(wb, sl, kc, True),
                                            xnff_hi[:, TF * kc:TF * kc + TF],
                                            xnff_lo[:, TF * kc:TF * kc + TF],
                                            kc == 0, kc == KC - 1)
                        t_wb_prev[sl] = mmu
                        ch = 4 * e2 + ic
                        s32 = scr32[:, TF * (ic % 2):TF * (ic % 2) + TF]
                        W("vector", mmu, t_s32[ic % 2],
                          last["scr"] if (e2 == 0 and ic == 0) else None)
                        tq1 = bd.do("vector", lambda E, sl=sl, s32=s32:
                                    E.tensor_scalar_max(s32, pmm[sl % 2][:, 0:TF], 0.0),
                                    "V", 1)
                        last["pmm%d" % (sl % 2)] = tq1
                        W("scalar", tq1)
                        tq2 = bd.do("scalar", lambda E, s32=s32:
                                    E.square(s32, s32), "S", 1)
                        W("vector", tq2, mcw)
                        tq3 = bd.do("vector", lambda E, s32=s32:
                                    E.tensor_tensor(s32, s32, p2[:, 0:TF], ALU.mult),
                                    "V", 1)
                        t_p2u = tq3
                        W("scalar", tq3)
                        tq4 = bd.do("scalar", lambda E, ch=ch, s32=s32:
                                    E.copy(scr_hi[:, TF * ch:TF * ch + TF], s32),
                                    "S", 1)
                        W("vector", tq4)
                        tre = bd.do("vector", lambda E, ch=ch, s32=s32:
                                    E.tensor_sub(scr_lo[:, TF * ch:TF * ch + TF],
                                                 s32,
                                                 scr_hi[:, TF * ch:TF * ch + TF]),
                                    "V", 1)
                        t_r2[ch] = tre
                        t_s32[ic % 2] = tre
                    last["p2"] = t_p2u
                for oc in range(8):
                    sl = oc % 2
                    dwd = load_w(wbig, sl, "B%d" % sl, wdown[oc, ep],
                                 sdn_s[oc, ep], False, t_wbig_prev[sl], sw_ffn_l[0])
                    W("tensor", dwd, last["pmm%d" % (sl % 2)])
                    mmd = None
                    for ch in range(8):
                        if oc == 0:
                            W("tensor", t_r2[ch])
                        mmd = mm_triple(pmm[sl % 2][:, 0:TF],
                                        wsl(wbig, sl, ch), wsl(wbig, sl, ch, True),
                                        scr_hi[:, TF * ch:TF * ch + TF],
                                        scr_lo[:, TF * ch:TF * ch + TF],
                                        ch == 0, ch == 7)
                    t_wbig_prev[sl] = mmd
                    W("vector", mmd, t_rout[oc])
                    trt = bd.do("vector", lambda E, oc=oc, sl=sl:
                                E.tensor_tensor(routed[:, TF * oc:TF * oc + TF],
                                                pmm[sl % 2][:, 0:TF],
                                                routed[:, TF * oc:TF * oc + TF],
                                                ALU.add), "V", 1)
                    t_rout[oc] = trt
                    last["pmm%d" % (sl % 2)] = trt
                last["scr"] = bd.tok("T")
            last["cwfm"] = bd.tok("T")

            if f + 1 < NF:
                last["xnff"] = bd.tok("T")
                t_xnff_next = ffn_prep_tr(lnsf_next)

            p3fin = [None] * 4
            for tsub in range(4):
                osl = tsub % 2
                W("sync", t_xres_prev[osl])
                dxs = bd.do("sync", lambda E, f=f, tsub=tsub, osl=osl:
                            E.dma_start(out=xres[:, 1024 * osl:1024 * osl + 1024],
                                        in_=x_res[512 * f + 128 * tsub:
                                                  512 * f + 128 * tsub + 128, :]),
                            "XR%d" % osl, 16)
                tfe = None
                pfin = [p3, p2]
                pfin_tok = [last["p3"], last["p2"]]
                for oc in range(8):
                    W("tensor", t_rout[oc] if tsub == 0 else None,
                      pfin_tok[oc % 2] if oc < 2 else None, p3fin[oc % 2])
                    tft = bd.do("tensor", lambda E, oc=oc, tsub=tsub:
                                E.transpose(pfin[oc % 2][:, 0:128],
                                            routed[:, TF * oc + 128 * tsub:
                                                   TF * oc + 128 * tsub + 128],
                                            ident[:]), "T", 1)
                    W("vector", tft, t_outp_prev[osl] if oc == 0 else None,
                      dxs if oc == 0 else None)
                    tfe = bd.do("vector", lambda E, oc=oc, osl=osl:
                                E.tensor_tensor(
                                    outp[:, 1024 * osl + 128 * oc:
                                         1024 * osl + 128 * oc + 128],
                                    pfin[oc % 2][:, 0:128],
                                    xres[:, 1024 * osl + 128 * oc:
                                         1024 * osl + 128 * oc + 128],
                                    ALU.add), "V", 1)
                    p3fin[oc % 2] = tfe
                last["p3"] = p3fin[0]
                last["p2"] = p3fin[1]
                t_xres_prev[osl] = tfe
                W("gpsimd", tfe)
                t_outp_prev[osl] = bd.do("gpsimd", lambda E, f=f, tsub=tsub, osl=osl:
                                         E.dma_start(
                                             out=out_d[512 * f + 128 * tsub:
                                                       512 * f + 128 * tsub + 128, :],
                                             in_=outp[:, 1024 * osl:1024 * osl + 1024]),
                                         "O%d" % (osl), 16)
            last["routed"] = bd.tok("T")
            if f + 1 < NF:
                t_xnff = t_xnff_next

        for k in ("I", "X", "W0", "W1", "W2", "W3", "WV0", "WV1", "B0", "B1",
                  "SHQ", "SHK", "PV",
                  "PV2", "O0", "O1", "F0", "F1", "SW0", "SW1", "XR0", "XR1",
                  "G0", "G1"):
            bd.W("gpsimd", (k, bd.cnt[k]))

        @block.sync
        def _(E):
            bd.replay("sync", E, sems)

        @block.gpsimd
        def _(E):
            bd.replay("gpsimd", E, sems)

        @block.tensor
        def _(E):
            bd.replay("tensor", E, sems)

        @block.vector
        def _(E):
            bd.replay("vector", E, sems)

        @block.scalar
        def _(E):
            bd.replay("scalar", E, sems)

    return nc


_NC = None


def _get_nc():
    global _NC
    if _NC is None:
        _NC = build_nc()
    return _NC


def _prep_shared(inputs):
    f32 = np.float32
    w_qkv = np.asarray(inputs["L0_attn_w_qkv"], f32)
    w_proj = np.asarray(inputs["L0_attn_w_proj"], f32)
    b_proj = np.asarray(inputs["L0_attn_b_proj"], f32)
    gate_w = np.asarray(inputs["L0_ffn_gate_weight"], f32)
    gate_b = np.asarray(inputs["L0_ffn_gate_bias"], f32)
    up = np.asarray(inputs["L0_ffn_up_weights"], f32)
    down = np.asarray(inputs["L0_ffn_down_weights"], f32)
    sup = np.asarray(inputs["L0_ffn_shared_up_weight"], f32)
    sdown = np.asarray(inputs["L0_ffn_shared_down_weight"], f32)

    C = np.ascontiguousarray
    sh = {}
    sh["wqk"] = C(w_qkv[:2048].reshape(16, 128, 8, 128).transpose(0, 3, 2, 1)
                  .reshape(16, 128, 1024))
    sh["wv"] = C(w_qkv[2048:].reshape(1024, 8, 128).transpose(1, 2, 0))
    sh["wproj"] = C(w_proj.reshape(8, 128, 8, 128).transpose(0, 3, 2, 1)
                    .reshape(8, 128, 1024))
    sh["wup"] = C(up.reshape(8, 4, 128, 8, 128).transpose(0, 1, 4, 3, 2)
                  .reshape(8, 4, 128, 1024))
    sh["wdown"] = C(down.reshape(4, 2, 8, 128, 4, 128).transpose(2, 0, 5, 1, 4, 3)
                    .reshape(8, 4, 128, 1024))
    sh["wsup"] = C(sup.reshape(16, 128, 8, 128).transpose(0, 3, 2, 1)
                   .reshape(16, 128, 1024))
    sh["wsdown"] = C(sdown.reshape(8, 128, 2, 8, 128).transpose(0, 2, 4, 3, 1)
                     .reshape(8, 2, 128, 1024))
    sh["wgate"] = C(gate_w.T.reshape(8, 128, 8).transpose(1, 0, 2).reshape(128, 64))
    sh["bproj"] = C(b_proj.reshape(8, 128).T)
    sh["gbias"] = C(np.broadcast_to(gate_b, (128, 8)))
    sh["identw"] = np.eye(128, dtype=f32)
    jpat = ((np.arange(16 * TA) % 128) // 16) % 8
    km = np.zeros((8, 16 * TA), f32)
    qm = np.zeros((8, 16 * TA), f32)
    for j0 in range(8):
        km[j0] = (jpat == j0).astype(f32)
        qm[j0] = -MASKC * (jpat != j0).astype(f32)
    sh["kmask"] = km
    sh["qmask"] = qm
    selmat = np.zeros((8, 1024), f32)
    for e in range(8):
        selmat[e, 128 * e:128 * e + 128] = 1.0
    sh["selm"] = selmat
    return sh


def kernel(**inputs):
    x = np.asarray(inputs["L0_attn_x"], np.float32)
    sh = _prep_shared(inputs)
    rr = 512 * np.arange(16)[:, None] + np.arange(128)[None, :]
    in_maps = []
    for c in range(8):
        b, jc = c // 4, c % 4
        m = dict(sh)
        m["x_att"] = np.ascontiguousarray(x[b, 2048 * jc:2048 * jc + 2048])
        m["x_res"] = np.ascontiguousarray(x[b][(rr + 128 * jc).ravel()])
        in_maps.append(m)
    nc = _get_nc()
    res = run_bass_kernel_spmd(nc, in_maps, core_ids=list(range(8)))
    out = np.empty((2, 8192, 1024), np.float32)
    for c in range(8):
        b, jc = c // 4, c % 4
        out[b][(rr + 128 * jc).ravel()] = res.results[c]["out"]
    return out
